# version 05
# baseline (speedup 1.0000x reference)
import math
import contextlib
import numpy as np
import ml_dtypes
import concourse.bass as bass
import concourse.mybir as mybir
from concourse.bass_utils import run_bass_kernel_spmd

F32 = mybir.dt.float32
BF16 = mybir.dt.bfloat16
AF = mybir.ActivationFunctionType
ALU = mybir.AluOpType
AX = mybir.AxisListType

D = 2048
DC = 16
INTER = 4096
NH = 64
HD = 64
NG = 8
NS = 128
GN = 1024
CONV = 6144
NCC = 48
T = 4096
PRE = 128
TTOT = T + PRE
Q = 128
NCH = 32
NCHE = 32
EPS = 1e-5
RT8I = 1.0 / math.sqrt(8.0)

HALVES = [(0, PRE + 2048, 17, True), (PRE + 2048, 2048, 16, False)]


class Ctr:
    __slots__ = ("h", "v")

    def __init__(self, h):
        self.h = h
        self.v = 0


def build_nc():
    nc = bass.Bass("TRN2", target_bir_lowering=False)

    x_in = nc.dram_tensor("x", [TTOT, D], F32, kind="ExternalInput").ap()
    wz_d = nc.dram_tensor("wz", [D, INTER], F32, kind="ExternalInput").ap()
    wx_d = nc.dram_tensor("wx", [D, INTER], F32, kind="ExternalInput").ap()
    wb_d = nc.dram_tensor("wb", [D, GN], F32, kind="ExternalInput").ap()
    wc_d = nc.dram_tensor("wc", [D, GN], F32, kind="ExternalInput").ap()
    wdt_d = nc.dram_tensor("wdt", [D, NH], F32, kind="ExternalInput").ap()
    convw_d = nc.dram_tensor("convw", [128, NCC * 4], F32, kind="ExternalInput").ap()
    convb_d = nc.dram_tensor("convb", [128, NCC], F32, kind="ExternalInput").ap()
    dtb_d = nc.dram_tensor("dtb", [NH, 1], F32, kind="ExternalInput").ap()
    alog_d = nc.dram_tensor("alog", [NH, 1], F32, kind="ExternalInput").ap()
    dvec_d = nc.dram_tensor("dvec", [1, NH], F32, kind="ExternalInput").ap()
    normw_d = nc.dram_tensor("normw", [128, 32], F32, kind="ExternalInput").ap()
    wout_d = nc.dram_tensor("wout", [INTER, D], F32, kind="ExternalInput").ap()
    fac_d = nc.dram_tensor("factors", [8, D, D], F32, kind="ExternalInput").ap()
    fbias_d = nc.dram_tensor("fbias", [1, D], F32, kind="ExternalInput").ap()
    flags_d = nc.dram_tensor("flags", [128, 2], F32, kind="ExternalInput").ap()
    out_d = nc.dram_tensor("out", [T, D], F32, kind="ExternalOutput").ap()

    wsum_dram = nc.dram_tensor("wsum_sc", [DC, 128, D], BF16).ap()
    xn_sp = nc.dram_tensor("xn_sp", [DC, 128, TTOT], BF16).ap()
    xconv_sp = nc.dram_tensor("xconv_sp", [NCH, NCC, 128, 128], BF16).ap()
    sgate_sp = nc.dram_tensor("sgate_sp", [NCH, 128, INTER], BF16).ap()
    y_sp = nc.dram_tensor("y_sp", [NCH, 128, INTER], BF16).ap()
    ecg_sp = nc.dram_tensor("ecg_sp", [NCH, 128, NH], F32).ap()
    gffn_sp = nc.dram_tensor("gffn_sp", [NCH, 128, D], BF16).ap()
    normed_sp = nc.dram_tensor("normed_sp", [NCH, 32, 128, 128], BF16).ap()
    cumflat_dr = nc.dram_tensor("cumflat", [1, NH * Q], F32).ap()
    cc_in = nc.dram_tensor("cc_in", [128, INTER], F32).ap()
    cc_out = nc.dram_tensor("cc_out", [128, INTER], F32).ap()

    id_np = np.eye(128, dtype=np.float32)
    ident_f = nc.inline_tensor(id_np, "ident_f").ap()
    ident_b = nc.inline_tensor(id_np.astype(ml_dtypes.bfloat16), "ident_b").ap()
    tri_np = np.triu(np.ones((128, 128), dtype=np.float32))
    tri_c = nc.inline_tensor(tri_np, "tri_c").ap()

    st = contextlib.ExitStack()
    with st:
        E = st.enter_context

        sL = Ctr(E(nc.semaphore("sL")))
        sFl = Ctr(E(nc.semaphore("sFl")))
        sSt = Ctr(E(nc.semaphore("sSt")))
        sYs = Ctr(E(nc.semaphore("sYs")))
        sEs = Ctr(E(nc.semaphore("sEs")))
        sO = Ctr(E(nc.semaphore("sO")))
        sP = Ctr(E(nc.semaphore("sP")))
        sV = Ctr(E(nc.semaphore("sV")))
        sA = Ctr(E(nc.semaphore("sA")))
        sG = Ctr(E(nc.semaphore("sG")))
        sC = Ctr(E(nc.semaphore("sC")))
        sD = [Ctr(E(nc.semaphore(f"sD{i}"))) for i in range(4)]
        sE = [Ctr(E(nc.semaphore(f"sE{i}"))) for i in range(2)]

        def IN(ins, c, k=1):
            c.v += k
            if k:
                ins.then_inc(c.h, k)
            return c.v

        def MARK(eng, c):
            eng.drain()
            c.v += 1
            eng.sem_inc(c.h, 1)
            return c.v

        identf = E(nc.sbuf_tensor("identf", [128, 128], F32))
        identb = E(nc.sbuf_tensor("identb", [128, 128], BF16))
        tri_s = E(nc.sbuf_tensor("tri_s", [128, 128], F32))
        convw_s = E(nc.sbuf_tensor("convw_s", [128, NCC * 4], F32))
        convb_s = E(nc.sbuf_tensor("convb_s", [128, NCC], F32))
        dtb_s = E(nc.sbuf_tensor("dtb_s", [NH, 1], F32))
        alog_s = E(nc.sbuf_tensor("alog_s", [NH, 1], F32))
        aneg_s = E(nc.sbuf_tensor("aneg_s", [NH, 1], F32))
        dbc_s = E(nc.sbuf_tensor("dbc_s", [128, NH], F32))
        flags_s = E(nc.sbuf_tensor("flags_s", [128, 2], F32))
        normw_s = E(nc.sbuf_tensor("normw_s", [128, 32], F32))
        constv = E(nc.sbuf_tensor("constv", [128, 6], F32))
        zero64 = E(nc.sbuf_tensor("zero64", [NH, Q], F32))
        mid = contextlib.ExitStack()
        M_ = mid.enter_context
        dt_s = M_(nc.sbuf_tensor("dt_s", [NH, T], F32))
        a_s = M_(nc.sbuf_tensor("a_s", [NH, T], F32))
        S_nd = M_(nc.sbuf_tensor("S_nd", [128, INTER], F32))
        S_bf = M_(nc.sbuf_tensor("S_bf", [128, INTER], BF16))
        dt_raw = M_(nc.sbuf_tensor("dt_raw", [NH, TTOT], F32))
        stash = M_(nc.sbuf_tensor("stash", [128, NCC * 6], F32))

        bA0 = sA.v
        with nc.Block() as blk:
            @blk.sync
            def _(sync):
                for dst, src in [
                    (identf[:, :], ident_f), (identb[:, :], ident_b),
                    (tri_s[:, :], tri_c), (convw_s[:, :], convw_d),
                    (convb_s[:, :], convb_d), (dtb_s[:, :], dtb_d),
                    (alog_s[:, :], alog_d), (flags_s[:, :], flags_d),
                    (normw_s[:, :], normw_d),
                    (dbc_s[:, :], dvec_d.broadcast_to([128, NH])),
                ]:
                    IN(sync.dma_start(dst, src), sL, 16)
                sync.wait_ge(sL.h, sL.v)

            initL = sL.v

            @blk.scalar
            def _(scalar):
                scalar.wait_ge(sL.h, initL)
                scalar.activation(alog_s[:, :], alog_s[:, :], AF.Exp)
                MARK(scalar, sA)

            @blk.vector
            def _(vector):
                vector.wait_ge(sL.h, initL)
                vector.memset(zero64[:, :], 0.0)
                vector.memset(constv[:, 0:1], EPS)
                vector.memset(constv[:, 1:2], -0.5)
                vector.memset(constv[:, 2:3], -1.0)
                vector.memset(constv[:, 3:4], 1.0)
                vector.memset(constv[:, 4:5], 1.0 / INTER)
                vector.memset(S_nd[:, :], 0.0)
                vector.memset(S_bf[:, :], 0.0)
                vector.wait_ge(sA.h, bA0 + 1)
                vector.tensor_scalar_mul(aneg_s[:, :], alog_s[:, :], -1.0)

        with contextlib.ExitStack() as ph:
            L_ = ph.enter_context
            fac_b = [L_(nc.sbuf_tensor(f"facb{i}", [128, D], F32)) for i in range(2)]
            facacc = L_(nc.sbuf_tensor("facacc", [128, D], F32))
            wsbf = L_(nc.sbuf_tensor("wsbf", [128, D], BF16))
            bV, bS = sV.v, sSt.v
            fac_v = []
            with nc.Block() as blk:
                @blk.sync
                def _(sync):
                    for dc in range(DC):
                        for r in range(8):
                            k = dc * 8 + r
                            if k >= 2:
                                dc2, r2 = divmod(k - 2, 8)
                                sync.wait_ge(sV.h, bV + dc2 * 9 + r2 + 1)
                            fac_v.append(IN(
                                sync.dma_start(fac_b[k % 2][:, :],
                                               fac_d[r, dc * 128:(dc + 1) * 128, :]),
                                sD[k % 2], 16))

                @blk.vector
                def _(vector):
                    for dc in range(DC):
                        if dc:
                            vector.wait_ge(sSt.h, bS + 16 * dc)
                        for r in range(8):
                            k = dc * 8 + r
                            vector.wait_ge(sD[k % 2].h, fac_v[k])
                            if r == 0:
                                IN(vector.tensor_copy(facacc[:, :], fac_b[k % 2][:, :]),
                                   sV)
                            else:
                                vector.drain()
                                IN(vector.tensor_add(facacc[:, :], facacc[:, :],
                                                     fac_b[k % 2][:, :]), sV)
                        vector.drain()
                        vector.tensor_copy(wsbf[:, :], facacc[:, :])
                        MARK(vector, sV)

                @blk.scalar
                def _(scalar):
                    for dc in range(DC):
                        scalar.wait_ge(sV.h, bV + 9 * dc + 9)
                        IN(scalar.dma_start(wsum_dram[dc, :, :], wsbf[:, :]), sSt, 16)
                    scalar.wait_ge(sSt.h, bS + 256)
            assert sV.v == bV + 144 and sSt.v == bS + 256

        tcnt = [0] * NCC

        for h, (C0, W, NCHK, HASPRE) in enumerate(HALVES):
            with contextlib.ExitStack() as ph:
                xn_fm = ph.enter_context(
                    nc.sbuf_tensor(f"xn_fm{h}", [128, DC * W], BF16))

                with contextlib.ExitStack() as lst:
                    L_ = lst.enter_context
                    xt = [L_(nc.sbuf_tensor(f"xt{h}_{i}", [128, D], F32)) for i in range(2)]
                    sq = L_(nc.sbuf_tensor(f"sq{h}", [128, D], F32))
                    xnt = L_(nc.sbuf_tensor(f"xnt{h}", [128, D], BF16))
                    s1 = L_(nc.sbuf_tensor(f"s1{h}", [128, 1], F32))
                    ssq = L_(nc.sbuf_tensor(f"ssq{h}", [128, 1], F32))
                    mean = L_(nc.sbuf_tensor(f"mean{h}", [128, 1], F32))
                    m2 = L_(nc.sbuf_tensor(f"m2{h}", [128, 1], F32))
                    var = L_(nc.sbuf_tensor(f"var{h}", [128, 1], F32))
                    lnl = L_(nc.sbuf_tensor(f"lnl{h}", [128, 1], F32))
                    rstd = L_(nc.sbuf_tensor(f"rstd{h}", [128, 1], F32))
                    mb = L_(nc.sbuf_tensor(f"mb{h}", [128, 1], F32))
                    ps_t = [L_(nc.psum_tensor(f"ps_t{h}_{i}", [128, 512], F32))
                            for i in range(2)]
                    bV, bA, bP = sV.v, sA.v, sP.v
                    NV, NA, NP = 10, 10, 16
                    xt_v = []
                    with nc.Block() as blk:
                        @blk.sync
                        def _(sync):
                            for i in range(NCHK):
                                if i >= 2:
                                    sync.wait_ge(sA.h, bA + NA * (i - 2) + 2)
                                xt_v.append(IN(sync.dma_start(
                                    xt[i % 2][:, :],
                                    x_in[C0 + i * 128: C0 + (i + 1) * 128, :]),
                                    sD[i % 2], 16))

                        @blk.tensor
                        def _(tensor):
                            for i in range(NCHK):
                                tensor.wait_ge(sA.h, bA + NA * i + 2)
                                for dc in range(DC):
                                    b = dc // 4
                                    if dc % 4 == 0:
                                        pb, pi = (b - 2, i) if b >= 2 else (b + 2, i - 1)
                                        if pi >= 0:
                                            lastev = 4 * (pb // 2) + 4
                                            if pb % 2 == 0:
                                                tensor.wait_ge(
                                                    sV.h, bV + NV * pi + 2 + lastev)
                                            else:
                                                tensor.wait_ge(
                                                    sA.h, bA + NA * pi + 2 + lastev)
                                    IN(tensor.matmul(
                                        ps_t[b % 2][:, 128 * (dc % 4):
                                                    128 * (dc % 4 + 1)],
                                        xnt[:, 128 * dc:128 * (dc + 1)],
                                        identb[:, :], start=True, stop=True), sP)

                        @blk.vector
                        def _(vector):
                            for i in range(NCHK):
                                vector.wait_ge(sD[i % 2].h, xt_v[i])
                                vector.reduce_sum(s1[:, :], xt[i % 2][:, :], axis=AX.X)
                                vector.scalar_tensor_tensor(
                                    sq[:, :], xt[i % 2][:, :], 1.0, xt[i % 2][:, :],
                                    ALU.bypass, ALU.mult, accum_out=ssq[:, :])
                                vector.drain()
                                vector.tensor_scalar_mul(mean[:, :], s1[:, :], 1.0 / D)
                                vector.drain()
                                vector.tensor_mul(m2[:, :], mean[:, :], mean[:, :])
                                vector.drain()
                                IN(vector.scalar_tensor_tensor(
                                    var[:, :], ssq[:, :], 1.0 / D, m2[:, :],
                                    ALU.mult, ALU.subtract), sV)
                                vector.wait_ge(sA.h, bA + NA * i + 1)
                                IN(vector.scalar_tensor_tensor(
                                    mb[:, :], mean[:, :], -1.0, rstd[:, :],
                                    ALU.mult, ALU.mult), sV)
                                for b in (0, 2):
                                    vector.wait_ge(sP.h,
                                                   bP + NP * i + 4 * (b + 1))
                                    for k in range(4):
                                        dc = 4 * b + k
                                        IN(vector.tensor_copy(
                                            xn_fm[:, dc * W + i * 128:
                                                  dc * W + (i + 1) * 128],
                                            ps_t[0][:, 128 * k:128 * (k + 1)]),
                                            sV)

                        @blk.scalar
                        def _(scalar):
                            for i in range(NCHK):
                                scalar.wait_ge(sV.h, bV + NV * i + 1)
                                scalar.activation(lnl[:, :], var[:, :], AF.Ln,
                                                  bias=constv[:, 0:1])
                                scalar.drain()
                                IN(scalar.activation(rstd[:, :], lnl[:, :], AF.Exp,
                                                     scale=constv[:, 1:2]), sA)
                                scalar.wait_ge(sV.h, bV + NV * i + 2)
                                scalar.drain()
                                IN(scalar.activation(xnt[:, :], xt[i % 2][:, :],
                                                     AF.Identity, bias=mb[:, :],
                                                     scale=rstd[:, :]), sA)
                                for b in (1, 3):
                                    scalar.wait_ge(sP.h,
                                                   bP + NP * i + 4 * (b + 1))
                                    for k in range(4):
                                        dc = 4 * b + k
                                        IN(scalar.copy(
                                            xn_fm[:, dc * W + i * 128:
                                                  dc * W + (i + 1) * 128],
                                            ps_t[1][:, 128 * k:128 * (k + 1)]),
                                            sA)
                    assert sV.v == bV + NV * NCHK and sA.v == bA + NA * NCHK
                    assert sP.v == bP + NP * NCHK

                if HASPRE:
                    tiles = [(0, 128)] + [(128 + 512 * k, 512) for k in range(4)]
                else:
                    tiles = [(512 * k, 512) for k in range(4)]
                NTI = len(tiles)
                with contextlib.ExitStack() as pst:
                    L_ = pst.enter_context
                    wtmp = [L_(nc.sbuf_tensor(f"wtmp{h}_{i}", [128, D], F32))
                            for i in range(2)]
                    wbf = [L_(nc.sbuf_tensor(f"wbf{h}_{i}", [128, D], BF16))
                           for i in range(2)]
                    acc_a = L_(nc.sbuf_tensor(f"acc_a{h}", [128, 512], F32))
                    acc_b = L_(nc.sbuf_tensor(f"acc_b{h}", [128, 512], F32))
                    xcbuf = [L_(nc.sbuf_tensor(f"xcbuf{h}_{i}", [128, 512], BF16))
                             for i in range(2)]
                    ps_x = [L_(nc.psum_tensor(f"ps_x{h}_{i}", [128, 512], F32))
                            for i in range(2)]
                    bV, bA, bP, bG, bS = sV.v, sA.v, sP.v, sG.v, sSt.v
                    wt_v = []
                    NXS = 16

                    def wsrc(cc):
                        if cc < 32:
                            return wx_d[:, 128 * cc:128 * (cc + 1)]
                        if cc < 40:
                            return wb_d[:, 128 * (cc - 32):128 * (cc - 31)]
                        return wc_d[:, 128 * (cc - 40):128 * (cc - 39)]

                    act_per_tile = [0 if (HASPRE and t == 0) else 2
                                    for t in range(NTI)]
                    actpre = [0]
                    for cc in range(NCC):
                        for t in range(NTI):
                            actpre.append(actpre[-1] + act_per_tile[t])

                    with nc.Block() as blk:
                        @blk.sync
                        def _(sync):
                            for dc in range(DC):
                                IN(sync.dma_start(
                                    xn_sp[dc, :, C0:C0 + W],
                                    xn_fm[:, dc * W:(dc + 1) * W]), sSt, 16)
                            for cc in range(NCC):
                                if cc >= 2:
                                    sync.wait_ge(sG.h, bG + cc - 1)
                                wt_v.append(IN(sync.dma_start(
                                    wtmp[cc % 2][:, :].rearrange(
                                        "p (dc f) -> p dc f", dc=DC),
                                    wsrc(cc).rearrange("(dc p) f -> p dc f", p=128)),
                                    sD[cc % 2], 16))

                        @blk.gpsimd
                        def _(gpsimd):
                            for cc in range(NCC):
                                gpsimd.wait_ge(sD[cc % 2].h, wt_v[cc])
                                if cc >= 2:
                                    gpsimd.wait_ge(sP.h, bP + (cc - 1) * NTI)
                                IN(gpsimd.tensor_copy(wbf[cc % 2][:, :],
                                                      wtmp[cc % 2][:, :]), sG)

                        @blk.tensor
                        def _(tensor):
                            for cc in range(NCC):
                                tensor.wait_ge(sG.h, bG + cc + 1)
                                for t, (tc0, tw) in enumerate(tiles):
                                    idx = cc * NTI + t
                                    if idx >= 2:
                                        tensor.wait_ge(sV.h, bV + idx - 1)
                                    pp = idx % 2
                                    for dc in range(DC):
                                        IN(tensor.matmul(
                                            ps_x[pp][:, 0:tw],
                                            wbf[cc % 2][:, 128 * dc:128 * (dc + 1)],
                                            xn_fm[:, dc * W + tc0:dc * W + tc0 + tw],
                                            start=(dc == 0), stop=(dc == 15)),
                                            sP, 1 if dc == 15 else 0)

                        @blk.scalar
                        def _(scalar):
                            xb = 0
                            for cc in range(NCC):
                                for t, (tc0, tw) in enumerate(tiles):
                                    idx = cc * NTI + t
                                    pp = idx % 2
                                    if HASPRE and t == 0:
                                        continue
                                    scalar.wait_ge(sP.h, bP + idx + 1)
                                    if idx >= 1:
                                        scalar.wait_ge(sV.h, bV + idx)
                                    IN(scalar.activation(
                                        acc_a[:, 0:tw], ps_x[pp][:, 0:tw],
                                        AF.Identity,
                                        scale=convw_s[:, cc * 4 + 3:cc * 4 + 4],
                                        bias=convb_s[:, cc:cc + 1]), sA)
                                    scalar.wait_ge(sV.h, bV + idx + 1)
                                    if xb >= 2:
                                        scalar.wait_ge(sE[xb % 2].h, sE[xb % 2].v)
                                    buf = xcbuf[xb % 2]
                                    scalar.activation(buf[:, 0:tw], acc_b[:, 0:tw],
                                                      AF.Silu)
                                    scalar.drain()
                                    gcol = C0 + tc0
                                    ch0 = (gcol - PRE) // 128
                                    nch = tw // 128
                                    IN(scalar.dma_start(
                                        xconv_sp[ch0:ch0 + nch, cc, :, :]
                                        .rearrange("c p i -> p c i"),
                                        buf[:, 0:tw].rearrange(
                                            "p (c i) -> p c i", c=nch)),
                                        sE[xb % 2], 16)
                                    sA.v += 1
                                    scalar.sem_inc(sA.h, 1)
                                    xb += 1
                            scalar.wait_ge(sE[0].h, sE[0].v) if cc == NCC - 1 else None
                            scalar.wait_ge(sE[1].h, sE[1].v) if cc == NCC - 1 else None

                        @blk.vector
                        def _(vector):
                            for cc in range(NCC):
                                for t, (tc0, tw) in enumerate(tiles):
                                    idx = cc * NTI + t
                                    pp = idx % 2
                                    par = tcnt[cc] % 2
                                    npar = (tcnt[cc] + 1) % 2
                                    sb0 = cc * 6 + par * 3
                                    nb0 = cc * 6 + npar * 3
                                    if HASPRE and t == 0:
                                        vector.wait_ge(sP.h, bP + idx + 1)
                                        vector.tensor_copy(stash[:, nb0:nb0 + 3],
                                                           ps_x[pp][:, tw - 3:tw])
                                        MARK(vector, sV)
                                    else:
                                        vector.wait_ge(sA.h, bA + actpre[idx] + 1)
                                        srcdst = [(acc_a, acc_b), (acc_b, acc_a),
                                                  (acc_a, acc_b)]
                                        for ki, (sft, k) in enumerate(
                                                [(1, 2), (2, 1), (3, 0)]):
                                            prev, cur = srcdst[ki]
                                            wcol = convw_s[:, cc * 4 + k:
                                                           cc * 4 + k + 1]
                                            vector.scalar_tensor_tensor(
                                                cur[:, sft:tw],
                                                ps_x[pp][:, 0:tw - sft], wcol,
                                                prev[:, sft:tw], ALU.mult, ALU.add)
                                            vector.scalar_tensor_tensor(
                                                cur[:, 0:sft],
                                                stash[:, sb0 + 3 - sft:sb0 + 3],
                                                wcol, prev[:, 0:sft],
                                                ALU.mult, ALU.add)
                                            vector.drain()
                                        vector.tensor_copy(stash[:, nb0:nb0 + 3],
                                                           ps_x[pp][:, tw - 3:tw])
                                        MARK(vector, sV)
                                    tcnt[cc] += 1
                    assert sP.v == bP + NCC * NTI and sV.v == bV + NCC * NTI
                    assert sG.v == bG + NCC

                with contextlib.ExitStack() as pst:
                    L_ = pst.enter_context
                    wztmp = [L_(nc.sbuf_tensor(f"wztmp{h}_{i}", [128, 2 * 1024], F32))
                             for i in range(2)]
                    wzbf = L_(nc.sbuf_tensor(f"wzbf{h}", [128, DC * 1024], BF16))
                    sgst = [L_(nc.sbuf_tensor(f"sgst{h}_{i}", [128, 1024], BF16))
                            for i in range(2)]
                    ps_g = [L_(nc.psum_tensor(f"ps_g{h}_{i}", [128, 512], F32))
                            for i in range(4)]
                    bA, bP, bG, bS = sA.v, sP.v, sG.v, sSt.v
                    NTC = 16
                    wz_v = []
                    with nc.Block() as blk:
                        @blk.sync
                        def _(sync):
                            for g2 in range(4):
                                for ld in range(8):
                                    k = g2 * 8 + ld
                                    if k >= 2:
                                        sync.wait_ge(sG.h, bG + k - 1)
                                    wz_v.append(IN(sync.dma_start(
                                        wztmp[k % 2][:, :].rearrange(
                                            "p (dc f) -> p dc f", dc=2),
                                        wz_d[256 * ld:256 * (ld + 1),
                                             1024 * g2:1024 * (g2 + 1)].rearrange(
                                            "(dc p) f -> p dc f", p=128)),
                                        sD[k % 2], 16))

                        @blk.gpsimd
                        def _(gpsimd):
                            for g2 in range(4):
                                for ld in range(8):
                                    k = g2 * 8 + ld
                                    gpsimd.wait_ge(sD[k % 2].h, wz_v[k])
                                    if g2 >= 1 and ld == 0:
                                        gpsimd.wait_ge(sP.h, bP + NTC * g2)
                                    IN(gpsimd.tensor_copy(
                                        wzbf[:, (2 * ld) * 1024:(2 * ld + 2) * 1024],
                                        wztmp[k % 2][:, :]), sG)

                        @blk.tensor
                        def _(tensor):
                            for g2 in range(4):
                                tensor.wait_ge(sG.h, bG + 8 * (g2 + 1))
                                for tc in range(NTC):
                                    it = g2 * NTC + tc
                                    if it >= 2:
                                        tensor.wait_ge(sA.h, bA + it - 1)
                                    lc0 = (PRE if HASPRE else 0) + 128 * tc
                                    pb = 2 * (tc % 2)
                                    for dc in range(DC):
                                        for fq in range(2):
                                            IN(tensor.matmul(
                                                ps_g[pb + fq][:, :],
                                                xn_fm[:, dc * W + lc0:
                                                      dc * W + lc0 + 128],
                                                wzbf[:, dc * 1024 + 512 * fq:
                                                     dc * 1024 + 512 * (fq + 1)],
                                                start=(dc == 0), stop=(dc == 15)),
                                                sP,
                                                1 if (dc == 15 and fq == 1) else 0)

                        @blk.scalar
                        def _(scalar):
                            for g2 in range(4):
                                for tc in range(NTC):
                                    it = g2 * NTC + tc
                                    scalar.wait_ge(sP.h, bP + it + 1)
                                    if it >= 2:
                                        scalar.wait_ge(sE[it % 2].h, sE[it % 2].v)
                                    pb = 2 * (tc % 2)
                                    sgb = sgst[it % 2]
                                    for fq in range(2):
                                        scalar.activation(
                                            sgb[:, 512 * fq:512 * (fq + 1)],
                                            ps_g[pb + fq][:, :], AF.Silu)
                                    scalar.drain()
                                    tcg = h * 16 + tc
                                    IN(scalar.dma_start(
                                        sgate_sp[tcg, :, 1024 * g2:1024 * (g2 + 1)],
                                        sgb[:, :]), sE[it % 2], 16)
                                    sA.v += 1
                                    scalar.sem_inc(sA.h, 1)
                            if g2 == 3:
                                scalar.wait_ge(sE[0].h, sE[0].v)
                                scalar.wait_ge(sE[1].h, sE[1].v)
                    assert sP.v == bP + 4 * NTC and sA.v == bA + 4 * NTC

                with contextlib.ExitStack() as pst:
                    L_ = pst.enter_context
                    wdtmp = L_(nc.sbuf_tensor(f"wdtmp{h}", [128, DC * NH], F32))
                    wdtbf = L_(nc.sbuf_tensor(f"wdtbf{h}", [128, DC * NH], BF16))
                    ps_dt = L_(nc.psum_tensor(f"ps_dt{h}", [NH, 512], F32))
                    bL, bV, bP, bG = sL.v, sV.v, sP.v, sG.v
                    with nc.Block() as blk:
                        @blk.sync
                        def _(sync):
                            IN(sync.dma_start(
                                wdtmp[:, :].rearrange("p (dc f) -> p dc f", dc=DC),
                                wdt_d.rearrange("(dc p) f -> p dc f", p=128)), sL, 16)
                            sync.wait_ge(sSt.h, sSt.v)

                        @blk.gpsimd
                        def _(gpsimd):
                            gpsimd.wait_ge(sL.h, bL + 16)
                            IN(gpsimd.tensor_copy(wdtbf[:, :], wdtmp[:, :]), sG)

                        @blk.tensor
                        def _(tensor):
                            tensor.wait_ge(sG.h, bG + 1)
                            for t, (tc0, tw) in enumerate(tiles):
                                if t >= 1:
                                    tensor.wait_ge(sV.h, bV + t)
                                for dc in range(DC):
                                    IN(tensor.matmul(
                                        ps_dt[:, 0:tw],
                                        wdtbf[:, NH * dc:NH * (dc + 1)],
                                        xn_fm[:, dc * W + tc0:dc * W + tc0 + tw],
                                        start=(dc == 0), stop=(dc == 15)),
                                        sP, 1 if dc == 15 else 0)

                        @blk.vector
                        def _(vector):
                            for t, (tc0, tw) in enumerate(tiles):
                                vector.wait_ge(sP.h, bP + t + 1)
                                IN(vector.tensor_copy(
                                    dt_raw[:, C0 + tc0:C0 + tc0 + tw],
                                    ps_dt[:, 0:tw]), sV)
                    assert sP.v == bP + NTI and sV.v == bV + NTI

        with contextlib.ExitStack() as ph:
            L_ = ph.enter_context
            zb = L_(nc.sbuf_tensor("zb", [NH, T], F32))
            ab = L_(nc.sbuf_tensor("ab", [NH, T], F32))
            eb = L_(nc.sbuf_tensor("eb", [NH, T], F32))
            mx = L_(nc.sbuf_tensor("mx", [NH, T], F32))
            bV, bA = sV.v, sA.v
            with nc.Block() as blk:
                @blk.vector
                def _(vector):
                    vector.tensor_scalar_add(zb[:, :], dt_raw[:, PRE:], dtb_s[:, :])
                    vector.drain()
                    IN(vector.tensor_scalar(mx[:, :], zb[:, :], 0.0, None, ALU.max),
                       sV)
                    vector.wait_ge(sA.h, bA + 1)
                    vector.tensor_add(dt_s[:, :], mx[:, :], eb[:, :])
                    vector.drain()
                    vector.tensor_scalar_mul(a_s[:, :], dt_s[:, :], aneg_s[:, :])

                @blk.scalar
                def _(scalar):
                    scalar.wait_ge(sV.h, bV + 1)
                    scalar.activation(ab[:, :], zb[:, :], AF.Abs)
                    scalar.drain()
                    scalar.activation(eb[:, :], ab[:, :], AF.Exp, scale=constv[0:64, 2:3])
                    scalar.drain()
                    scalar.activation(eb[:, :], eb[:, :], AF.Ln, bias=constv[0:64, 3:4])
                    MARK(scalar, sA)

        with contextlib.ExitStack() as ph:
            L_ = ph.enter_context
            xc = [L_(nc.sbuf_tensor(f"xc{i}", [128, NCC * 128], BF16))
                  for i in range(2)]
            X_tm = L_(nc.sbuf_tensor("X_tm", [128, INTER], BF16))
            Xt = L_(nc.sbuf_tensor("Xt", [128, INTER], BF16))
            Xtw = L_(nc.sbuf_tensor("Xtw", [128, INTER], BF16))
            btm_sb = L_(nc.sbuf_tensor("btm_sb", [128, 1024], BF16))
            bcast = L_(nc.sbuf_tensor("bcast", [128, NH * Q], F32))
            Ein = L_(nc.sbuf_tensor("Ein", [128, 8 * Q], F32))
            Ebf = L_(nc.sbuf_tensor("Ebf", [128, 8 * Q], BF16))
            Wp = L_(nc.sbuf_tensor("Wp", [128, 8 * Q], BF16))
            gttri = L_(nc.sbuf_tensor("gttri", [128, 2 * Q], BF16))
            cum_fm = L_(nc.sbuf_tensor("cum_fm", [NH, Q], F32))
            cum_tm = L_(nc.sbuf_tensor("cum_tm", [128, NH], F32))
            dtb_tm = L_(nc.sbuf_tensor("dtb_tm", [128, NH], BF16))
            wdecay = L_(nc.sbuf_tensor("wdecay", [NH, Q], F32))
            wdec_tm = L_(nc.sbuf_tensor("wdec_tm", [128, NH], BF16))
            ecum = L_(nc.sbuf_tensor("ecum", [128, NH], F32))
            lam_bc = L_(nc.sbuf_tensor("lam_bc", [128, NH], F32))
            lprod = [L_(nc.sbuf_tensor(f"lprod{i}", [128, NH], F32))
                     for i in range(2)]
            ecg_t = L_(nc.sbuf_tensor("ecg_t", [128, NH], F32))
            Y_sb = L_(nc.sbuf_tensor("Y_sb", [128, INTER], F32))
            y_bf = L_(nc.sbuf_tensor("y_bf", [128, INTER], BF16))
            t1g = L_(nc.sbuf_tensor("t1g", [128, 512], F32))
            tgP = L_(nc.sbuf_tensor("tgP", [128, 512], F32))
            ps_tf = L_(nc.psum_tensor("ps_tf", [128, 320], F32))
            ps_tx = [L_(nc.psum_tensor(f"ps_tx{i}", [128, 512], F32))
                     for i in range(2)]
            ps_gt = [L_(nc.psum_tensor(f"ps_gt{i}", [128, 128], F32))
                     for i in range(2)]
            ps_yi = L_(nc.psum_tensor("ps_yi", [128, 512], F32))
            ps_ya = L_(nc.psum_tensor("ps_ya", [128, 512], F32))
            ps_cb = L_(nc.psum_tensor("ps_cb", [128, 512], F32))

            bL, bV, bA, bP, bG = sL.v, sV.v, sA.v, sP.v, sG.v
            bFl, bYs, bEs = sFl.v, sYs.v, sEs.v
            stpre = sSt.v
            NPch, NVch, NAch, NGch = 75, 34, 51, 49

            def pP(j, name, g=0, k=0):
                basei = {"cumtr": 1, "dttr": 2, "wdtr": 3}
                if name in basei:
                    return bP + NPch * j + basei[name]
                off = {"gt": 1, "xtr": 2 + k, "btr": 6, "inter": 7, "intra": 8,
                       "ctrb": 9}
                return bP + NPch * j + 3 + 9 * g + off[name]

            def pV(j, name, g=0):
                if name == "cum":
                    return bV + NVch * j + 1
                off = {"gttri": 1, "mask": 2, "t1y": 3, "rec": 4}
                if name in off:
                    return bV + NVch * j + 1 + 4 * g + off[name]
                return bV + NVch * j + 34

            def pA(j, name, g=0, k=0):
                basei = {"wdecay": 1, "evacall": 2, "ecumlam": 3}
                if name in basei:
                    return bA + NAch * j + basei[name]
                if name == "xev":
                    return bA + NAch * j + 3 + 6 * g + k + 1
                return bA + NAch * j + 3 + 6 * g + 6

            def pG(j, name, g=0):
                if name in ("xs", "xws", "mask", "wp", "sl"):
                    off = {"xs": 1, "xws": 2, "mask": 3, "wp": 4, "sl": 5}
                    return bG + NGch * j + 5 * g + off[name]
                if name == "sbf":
                    return bG + NGch * j + 40 + g + 1
                if name == "dterm":
                    return bG + NGch * j + 49

            xcload_v = []
            repl_v = []

            with nc.Block() as blk:
                @blk.sync
                def _(sync):
                    sync.wait_ge(sSt.h, stpre)
                    xcload_v.append(IN(sync.dma_start(
                        xc[0][:, :].rearrange("p (cc i) -> p cc i", cc=NCC),
                        xconv_sp[0, :, :, :].rearrange("cc p i -> p cc i")),
                        sD[0], 16))
                    for j in range(NCHE):
                        if j + 1 < NCHE:
                            if j >= 1:
                                sync.wait_ge(sP.h, bP + NPch * j)
                            xcload_v.append(IN(sync.dma_start(
                                xc[(j + 1) % 2][:, :].rearrange(
                                    "p (cc i) -> p cc i", cc=NCC),
                                xconv_sp[j + 1, :, :, :].rearrange(
                                    "cc p i -> p cc i")), sD[(j + 1) % 2], 16))
                        sync.wait_ge(sV.h, pV(j, "cum"))
                        if j >= 1:
                            sync.wait_ge(sD[2].h, repl_v[j - 1])
                        IN(sync.dma_start(cumflat_dr, cum_fm[:, :]), sFl, 16)
                        sync.wait_ge(sFl.h, bFl + 16 * (j + 1))
                        if j >= 1:
                            sync.wait_ge(sV.h, pV(j - 1, "mask", 7))
                            sync.wait_ge(sG.h, pG(j - 1, "mask", 7))
                            sync.wait_ge(sA.h, pA(j - 1, "ecumlam"))
                        repl_v.append(IN(sync.dma_start(
                            bcast[:, :],
                            cumflat_dr.broadcast_to([128, NH * Q])), sD[2], 16))

                @blk.tensor
                def _(tensor):
                    for j in range(NCHE):
                        tensor.wait_ge(sV.h, pV(j, "cum"))
                        if j >= 1:
                            tensor.wait_ge(sA.h, pA(j - 1, "xev", 7, 4))
                        IN(tensor.matmul(ps_tf[:, 0:64], cum_fm[:, :],
                                           identf[0:64, 0:64],
                                           start=True, stop=True), sP)
                        IN(tensor.matmul(ps_tf[:, 64:128],
                                         dt_s[:, Q * j:Q * (j + 1)],
                                         identf[0:64, 0:64],
                                         start=True, stop=True), sP)
                        tensor.wait_ge(sA.h, pA(j, "wdecay"))
                        IN(tensor.matmul(ps_tf[:, 128:192], wdecay[:, :],
                                         identf[0:64, 0:64],
                                         start=True, stop=True), sP)
                        tensor.wait_ge(sD[j % 2].h, xcload_v[j])
                        for g in range(NG):
                            if j >= 1 or g >= 2:
                                jj, gg = (j, g - 2) if g >= 2 else (j - 1, g + 6)
                                tensor.wait_ge(sV.h, pV(jj, "gttri", gg))
                            IN(tensor.matmul(
                                ps_gt[g % 2][:, :],
                                xc[j % 2][:, (32 + g) * 128:(33 + g) * 128],
                                xc[j % 2][:, (40 + g) * 128:(41 + g) * 128],
                                start=True, stop=True), sP)
                            for k in range(4):
                                if k == 0:
                                    if g >= 2:
                                        tensor.wait_ge(sA.h,
                                                       pA(j, "xev", g - 2, 3))
                                    elif j >= 1:
                                        tensor.wait_ge(sA.h,
                                                       pA(j - 1, "xev", g + 6, 3))
                                src_cc = 4 * g + k
                                IN(tensor.matmul(
                                    ps_tx[g % 2][:, 128 * k:128 * (k + 1)],
                                    xc[j % 2][:, src_cc * 128:(src_cc + 1) * 128],
                                    identb[:, :], start=True, stop=True), sP)
                            if g >= 1:
                                tensor.wait_ge(sA.h, pA(j, "xev", g - 1, 4))
                            else:
                                tensor.wait_ge(sA.h, pA(j, "evacall"))
                                if j >= 1:
                                    tensor.wait_ge(sA.h, pA(j - 1, "xev", 7, 4))
                            IN(tensor.matmul(
                                ps_tf[:, 192:320],
                                xc[j % 2][:, (32 + g) * 128:(33 + g) * 128],
                                identb[:, :], start=True, stop=True), sP)
                            if j >= 1:
                                tensor.wait_ge(sG.h, pG(j - 1, "sbf", g))
                            if g >= 1:
                                tensor.wait_ge(sV.h, pV(j, "t1y", g - 1))
                            elif j >= 1:
                                tensor.wait_ge(sV.h, pV(j - 1, "t1y", 7))
                            IN(tensor.matmul(
                                ps_yi[:, :],
                                xc[j % 2][:, (40 + g) * 128:(41 + g) * 128],
                                S_bf[:, 512 * g:512 * (g + 1)],
                                start=True, stop=True), sP)
                            tensor.wait_ge(sG.h, pG(j, "wp", g))
                            for jj in range(8):
                                IN(tensor.matmul(
                                    ps_ya[:, 64 * jj:64 * (jj + 1)],
                                    Wp[:, 128 * jj:128 * (jj + 1)],
                                    Xt[:, 512 * g + 64 * jj:
                                       512 * g + 64 * (jj + 1)],
                                    start=True, stop=True),
                                    sP, 1 if jj == 7 else 0)
                            tensor.wait_ge(sG.h, pG(j, "xws", g))
                            tensor.wait_ge(sA.h, pA(j, "xev", g, 4))
                            if g >= 1:
                                tensor.wait_ge(sV.h, pV(j, "rec", g - 1))
                            elif j >= 1:
                                tensor.wait_ge(sV.h, pV(j - 1, "rec", 7))
                            for p2 in range(4):
                                IN(tensor.matmul(
                                    ps_cb[:, 128 * p2:128 * (p2 + 1)],
                                    btm_sb[:, 128 * g:128 * (g + 1)],
                                    Xtw[:, 512 * g + 128 * p2:
                                        512 * g + 128 * (p2 + 1)],
                                    start=True, stop=True),
                                    sP, 1 if p2 == 3 else 0)

                @blk.vector
                def _(vector):
                    vector.memset(lprod[0][:, :], 1.0)
                    for j in range(NCHE):
                        if j >= 1:
                            vector.wait_ge(sFl.h, bFl + 16 * j)
                            vector.wait_ge(sP.h, pP(j - 1, "wdtr"))
                        vector.tensor_tensor_scan(
                            cum_fm[:, :], a_s[:, Q * j:Q * (j + 1)], zero64[:, :],
                            0.0, ALU.add, ALU.add)
                        MARK(vector, sV)
                        for g in range(NG):
                            vector.wait_ge(sP.h, pP(j, "gt", g))
                            if g >= 2:
                                vector.wait_ge(sG.h, pG(j, "wp", g - 2))
                            elif j >= 1:
                                vector.wait_ge(sG.h, pG(j - 1, "wp", g + 6))
                            IN(vector.tensor_mul(
                                gttri[:, 128 * (g % 2):128 * (g % 2 + 1)],
                                ps_gt[g % 2][:, :],
                                tri_s[:, :]), sV)
                            vector.wait_ge(sD[2].h, repl_v[j])
                            vector.wait_ge(sA.h, pA(j, "evacall"))
                            if g >= 1:
                                vector.wait_ge(sA.h, pA(j, "E", g - 1))
                            elif j >= 1:
                                vector.wait_ge(sA.h, pA(j - 1, "E", 7))
                            for jj in range(4):
                                hh = 8 * g + jj
                                vector.tensor_scalar(
                                    Ein[:, 128 * jj:128 * (jj + 1)],
                                    bcast[:, 128 * hh:128 * (hh + 1)],
                                    cum_tm[:, hh:hh + 1], 0.0,
                                    ALU.subtract, ALU.min)
                            MARK(vector, sV)
                            vector.wait_ge(sP.h, pP(j, "intra", g))
                            vector.wait_ge(sA.h, pA(j, "ecumlam"))
                            if g == 0 and j >= 1:
                                vector.wait_ge(sG.h, pG(j - 1, "dterm"))
                            vector.tensor_mul(
                                t1g[:, :].rearrange("p (a b) -> p a b", a=8),
                                ps_yi[:, :].rearrange("p (a b) -> p a b", a=8),
                                ecum[:, 8 * g:8 * (g + 1)].unsqueeze(2)
                                .broadcast_to([128, 8, 64]))
                            vector.drain()
                            IN(vector.tensor_add(
                                Y_sb[:, 512 * g:512 * (g + 1)], t1g[:, :],
                                ps_ya[:, :]), sV)
                            vector.wait_ge(sP.h, pP(j, "ctrb", g))
                            vector.wait_ge(sG.h, pG(j, "sl", g))
                            IN(vector.tensor_add(
                                S_nd[:, 512 * g:512 * (g + 1)], tgP[:, :],
                                ps_cb[:, :]), sV)
                        vector.wait_ge(sA.h, pA(j, "ecumlam"))
                        if j >= 1:
                            vector.wait_ge(sEs.h, bEs + 16 * j)
                        vector.tensor_mul(ecg_t[:, :], ecum[:, :],
                                          lprod[j % 2][:, :])
                        vector.drain()
                        vector.tensor_mul(lprod[(j + 1) % 2][:, :],
                                          lprod[j % 2][:, :], lam_bc[:, :])
                        MARK(vector, sV)

                @blk.scalar
                def _(scalar):
                    for j in range(NCHE):
                        scalar.wait_ge(sV.h, pV(j, "cum"))
                        IN(scalar.activation(wdecay[:, :], cum_fm[:, :], AF.Exp,
                                             scale=constv[0:64, 2:3],
                                             bias=cum_fm[:, Q - 1:Q]), sA)
                        scalar.wait_ge(sP.h, pP(j, "wdtr"))
                        scalar.copy(cum_tm[:, :], ps_tf[:, 0:64])
                        scalar.copy(dtb_tm[:, :], ps_tf[:, 64:128])
                        scalar.copy(wdec_tm[:, :], ps_tf[:, 128:192])
                        MARK(scalar, sA)
                        scalar.wait_ge(sD[2].h, repl_v[j])
                        scalar.drain()
                        scalar.activation(ecum[:, :], cum_tm[:, :], AF.Exp)
                        scalar.activation(lam_bc[:, :], bcast[:, Q - 1::Q], AF.Exp)
                        MARK(scalar, sA)
                        if j >= 1:
                            scalar.wait_ge(sP.h, bP + NPch * j)
                            scalar.wait_ge(sG.h, pG(j - 1, "dterm"))
                        for g in range(NG):
                            scalar.wait_ge(sP.h, pP(j, "btr", g))
                            for k in range(5):
                                if k == 4:
                                    IN(scalar.copy(
                                        btm_sb[:, 128 * g:128 * (g + 1)],
                                        ps_tf[:, 192:320]), sA)
                                else:
                                    cc = 4 * g + k
                                    IN(scalar.copy(
                                        X_tm[:, 128 * cc:128 * (cc + 1)],
                                        ps_tx[g % 2][:, 128 * k:128 * (k + 1)]),
                                        sA)
                            scalar.wait_ge(sV.h, pV(j, "mask", g))
                            scalar.wait_ge(sG.h, pG(j, "mask", g))
                            if g >= 1:
                                scalar.wait_ge(sG.h, pG(j, "wp", g - 1))
                            elif j >= 1:
                                scalar.wait_ge(sG.h, pG(j - 1, "wp", 7))
                            IN(scalar.activation(Ebf[:, :], Ein[:, :], AF.Exp), sA)

                @blk.gpsimd
                def _(gpsimd):
                    for j in range(NCHE):
                        for g in range(NG):
                            gpsimd.wait_ge(sA.h, pA(j, "xev", g, 3))
                            gpsimd.wait_ge(sA.h, pA(j, "evacall"))
                            if g == 0 and j >= 1:
                                gpsimd.wait_ge(sP.h, bP + NPch * j)
                            IN(gpsimd.tensor_mul(
                                Xt[:, 512 * g:512 * (g + 1)].rearrange(
                                    "p (a b) -> p a b", a=8),
                                X_tm[:, 512 * g:512 * (g + 1)].rearrange(
                                    "p (a b) -> p a b", a=8),
                                dtb_tm[:, 8 * g:8 * (g + 1)].unsqueeze(2)
                                .broadcast_to([128, 8, 64])), sG)
                            gpsimd.wait_ge(sA.h, pA(j, "evacall"))
                            gpsimd.drain()
                            IN(gpsimd.tensor_mul(
                                Xtw[:, 512 * g:512 * (g + 1)].rearrange(
                                    "p (a b) -> p a b", a=8),
                                Xt[:, 512 * g:512 * (g + 1)].rearrange(
                                    "p (a b) -> p a b", a=8),
                                wdec_tm[:, 8 * g:8 * (g + 1)].unsqueeze(2)
                                .broadcast_to([128, 8, 64])), sG)
                            gpsimd.wait_ge(sD[2].h, repl_v[j])
                            if g >= 1:
                                gpsimd.wait_ge(sA.h, pA(j, "E", g - 1))
                            elif j >= 1:
                                gpsimd.wait_ge(sA.h, pA(j - 1, "E", 7))
                            for jj in range(4, 8):
                                hh = 8 * g + jj
                                gpsimd.tensor_scalar(
                                    Ein[:, 128 * jj:128 * (jj + 1)],
                                    bcast[:, 128 * hh:128 * (hh + 1)],
                                    cum_tm[:, hh:hh + 1], 0.0,
                                    ALU.subtract, ALU.min)
                            MARK(gpsimd, sG)
                            gpsimd.wait_ge(sA.h, pA(j, "E", g))
                            gpsimd.wait_ge(sV.h, pV(j, "gttri", g))
                            if g >= 1:
                                gpsimd.wait_ge(sP.h, pP(j, "intra", g - 1))
                            elif j >= 1:
                                gpsimd.wait_ge(sP.h, pP(j - 1, "intra", 7))
                            IN(gpsimd.tensor_mul(
                                Wp[:, :].rearrange("p (a b) -> p a b", a=8),
                                Ebf[:, :].rearrange("p (a b) -> p a b", a=8),
                                gttri[:, 128 * (g % 2):128 * (g % 2 + 1)]
                                .unsqueeze(1).broadcast_to([128, 8, 128])), sG)
                            gpsimd.wait_ge(sA.h, pA(j, "ecumlam"))
                            if g >= 1:
                                gpsimd.wait_ge(sV.h, pV(j, "rec", g - 1))
                            elif j >= 1:
                                gpsimd.wait_ge(sV.h, pV(j - 1, "rec", 7))
                            IN(gpsimd.tensor_mul(
                                tgP[:, :].rearrange("p (a b) -> p a b", a=8),
                                S_nd[:, 512 * g:512 * (g + 1)].rearrange(
                                    "p (a b) -> p a b", a=8),
                                lam_bc[:, 8 * g:8 * (g + 1)].unsqueeze(2)
                                .broadcast_to([128, 8, 64])), sG)
                        for g in range(NG):
                            gpsimd.wait_ge(sV.h, pV(j, "rec", g))
                            gpsimd.wait_ge(sP.h, pP(j, "inter", g))
                            IN(gpsimd.tensor_copy(
                                S_bf[:, 512 * g:512 * (g + 1)],
                                S_nd[:, 512 * g:512 * (g + 1)]), sG)
                        gpsimd.wait_ge(sV.h, pV(j, "ecg"))
                        IN(gpsimd.dma_start(ecg_sp[j, :, :], ecg_t[:, :]), sEs, 16)
                        gpsimd.wait_ge(sV.h, pV(j, "t1y", 7))
                        gpsimd.wait_ge(sP.h, bP + NPch * (j + 1))
                        if j >= 1:
                            gpsimd.wait_ge(sYs.h, bYs + 16 * j)
                        gpsimd.drain()
                        gpsimd.tensor_mul(
                            Xt[:, :].rearrange("p (a b) -> p a b", a=NH),
                            X_tm[:, :].rearrange("p (a b) -> p a b", a=NH),
                            dbc_s[:, :].unsqueeze(2).broadcast_to([128, NH, 64]))
                        gpsimd.drain()
                        gpsimd.tensor_add(y_bf[:, :], Y_sb[:, :], Xt[:, :])
                        gpsimd.drain()
                        IN(gpsimd.dma_start(y_sp[j, :, :], y_bf[:, :]), sYs, 16)
                        sG.v += 1
                        gpsimd.sem_inc(sG.h, 1)

            assert sP.v == bP + NPch * NCHE
            assert sV.v == bV + NVch * NCHE
            assert sA.v == bA + NAch * NCHE
            assert sG.v == bG + NGch * NCHE

            bV2, bSt2, bL2 = sV.v, sSt.v, sL.v
            with nc.Block() as blk:
                @blk.vector
                def _(vector):
                    vector.tensor_scalar_mul(Y_sb[:, :], S_nd[:, :], flags_s[:, 0:1])
                    MARK(vector, sV)
                    vector.wait_ge(sL.h, bL2 + 16)
                    vector.drain()
                    IN(vector.tensor_scalar_mul(S_nd[:, :], S_nd[:, :],
                                                flags_s[:, 1:2]), sV)

                @blk.gpsimd
                def _(gpsimd):
                    gpsimd.wait_ge(sSt.h, bSt2 + 16)
                    IN(gpsimd.collective_compute(
                        "AllReduce", ALU.add,
                        replica_groups=[[0, 1], [2, 3], [4, 5], [6, 7]],
                        ins=[cc_in], outs=[cc_out]), sC, 1)
                    gpsimd.wait_ge(sV.h, bV2 + 2)
                    IN(gpsimd.tensor_copy(S_bf[:, :], S_nd[:, :]), sG)

                @blk.sync
                def _(sync):
                    sync.wait_ge(sV.h, bV2 + 1)
                    IN(sync.dma_start(cc_in, Y_sb[:, :]), sSt, 16)
                    sync.wait_ge(sC.h, 1)
                    IN(sync.dma_start(S_nd[:, :], cc_out), sL, 16)

        with contextlib.ExitStack() as ph:
            L_ = ph.enter_context
            ybuf = [L_(nc.sbuf_tensor(f"ybuf{i}", [128, INTER], BF16))
                    for i in range(2)]
            sgbuf = [L_(nc.sbuf_tensor(f"sgbuf{i}", [128, INTER], BF16))
                     for i in range(2)]
            ecgb = [L_(nc.sbuf_tensor(f"ecgb{i}", [128, NH], F32)) for i in range(2)]
            cbuf = [L_(nc.sbuf_tensor(f"cbuf{i}", [128, NG * 128], BF16))
                    for i in range(2)]
            corr = L_(nc.sbuf_tensor("corr", [128, INTER], F32))
            g0 = L_(nc.sbuf_tensor("g0", [128, INTER], F32))
            scr5 = L_(nc.sbuf_tensor("scr5", [128, 512], F32))
            ssqc = L_(nc.sbuf_tensor("ssqc", [128, NG], F32))
            ssq2 = L_(nc.sbuf_tensor("ssq2", [128, 1], F32))
            lnl2 = L_(nc.sbuf_tensor("lnl2", [128, 1], F32))
            rstd2 = L_(nc.sbuf_tensor("rstd2", [128, 1], F32))
            nrm = L_(nc.sbuf_tensor("nrm", [128, INTER], BF16))
            nst = L_(nc.sbuf_tensor("nst", [128, INTER], BF16))
            ps_co = [L_(nc.psum_tensor(f"ps_co{i}", [128, 512], F32))
                     for i in range(2)]
            ps_tn = [L_(nc.psum_tensor(f"ps_tn{i}", [128, 512], F32))
                     for i in range(2)]
            bV, bA, bP, bG, bS = sV.v, sA.v, sP.v, sG.v, sSt.v
            ys_tot, es_tot, st_tot = sYs.v, sEs.v, sSt.v
            xe_tot = [sE[0].v, sE[1].v]
            corr_v = []
            NPc, NVc, NAc, NGc = 40, 33, 19, 8

            def cP(j, name, g=0, k=0):
                if name == "corr":
                    return bP + NPc * j + g + 1
                return bP + NPc * j + 8 + k + 1

            def cV(j, name, g=0, k=0):
                if name == "corrs":
                    return bV + NVc * j + g + 1
                if name == "sumsq":
                    return bV + NVc * j + 8 + g + 1
                if name == "red":
                    return bV + NVc * j + 17
                return bV + NVc * j + 17 + k + 1

            def cA(j, name, k=0):
                if name == "rstd":
                    return bA + NAc * j + 1
                if name == "normed":
                    return bA + NAc * j + 2
                if name == "ev":
                    return bA + NAc * j + 2 + k + 1
                return bA + NAc * j + NAc

            def cG(j, g):
                return bG + NGc * j + g + 1

            with nc.Block() as blk:
                @blk.sync
                def _(sync):
                    sync.wait_ge(sYs.h, ys_tot)
                    sync.wait_ge(sEs.h, es_tot)
                    sync.wait_ge(sSt.h, st_tot)
                    sync.wait_ge(sE[0].h, xe_tot[0])
                    sync.wait_ge(sE[1].h, xe_tot[1])
                    for j in range(NCHE):
                        if j >= 2:
                            sync.wait_ge(sG.h, cG(j - 2, 7))
                            sync.wait_ge(sV.h, cV(j - 2, "red"))
                            sync.wait_ge(sP.h, cP(j - 2, "corr", 7))
                        IN(sync.dma_start(ybuf[j % 2][:, :], y_sp[j, :, :]),
                           sD[j % 2], 16)
                        IN(sync.dma_start(sgbuf[j % 2][:, :], sgate_sp[j, :, :]),
                           sD[j % 2], 16)
                        IN(sync.dma_start(ecgb[j % 2][:, :], ecg_sp[j, :, :]),
                           sD[j % 2], 16)
                        corr_v.append(IN(sync.dma_start(
                            cbuf[j % 2][:, :].rearrange("p (cc i) -> p cc i", cc=NG),
                            xconv_sp[j, 40:48, :, :].rearrange("cc p i -> p cc i")),
                            sD[j % 2], 16))

                @blk.tensor
                def _(tensor):
                    for j in range(NCHE):
                        tensor.wait_ge(sD[j % 2].h, corr_v[j])
                        for g in range(NG):
                            if j >= 1 or g >= 2:
                                jj, gg = (j, g - 2) if g >= 2 else (j - 1, g + 6)
                                tensor.wait_ge(sV.h, cV(jj, "corrs", gg))
                            IN(tensor.matmul(
                                ps_co[g % 2][:, :],
                                cbuf[j % 2][:, 128 * g:128 * (g + 1)],
                                S_bf[:, 512 * g:512 * (g + 1)],
                                start=True, stop=True), sP)
                        tensor.wait_ge(sA.h, cA(j, "normed"))
                        for k in range(32):
                            b = k // 4
                            if k % 4 == 0:
                                pb, pj = (b - 2, j) if b >= 2 else (b + 6, j - 1)
                                if pj >= 0:
                                    lastev = 4 * (pb // 2) + 3
                                    if pb % 2 == 0:
                                        tensor.wait_ge(sV.h,
                                                       cV(pj, "ev", 0, lastev))
                                    else:
                                        tensor.wait_ge(sA.h, cA(pj, "ev", lastev))
                            IN(tensor.matmul(
                                ps_tn[b % 2][:, 128 * (k % 4):128 * (k % 4 + 1)],
                                nrm[:, 128 * k:128 * (k + 1)], identb[:, :],
                                start=True, stop=True), sP)

                @blk.vector
                def _(vector):
                    for j in range(NCHE):
                        for g in range(NG):
                            vector.wait_ge(sP.h, cP(j, "corr", g))
                            if j >= 1:
                                vector.wait_ge(sG.h, cG(j - 1, g))
                            IN(vector.tensor_mul(
                                corr[:, 512 * g:512 * (g + 1)].rearrange(
                                    "p (a b) -> p a b", a=8),
                                ps_co[g % 2][:, :].rearrange(
                                    "p (a b) -> p a b", a=8),
                                ecgb[j % 2][:, 8 * g:8 * (g + 1)].unsqueeze(2)
                                .broadcast_to([128, 8, 64])), sV)
                        for g in range(NG):
                            vector.wait_ge(sG.h, cG(j, g))
                            vector.drain()
                            IN(vector.scalar_tensor_tensor(
                                scr5[:, :], g0[:, 512 * g:512 * (g + 1)], 1.0,
                                g0[:, 512 * g:512 * (g + 1)],
                                ALU.bypass, ALU.mult,
                                accum_out=ssqc[:, g:g + 1]), sV)
                        vector.drain()
                        IN(vector.reduce_sum(ssq2[:, :], ssqc[:, :], axis=AX.X), sV)
                        first = True
                        for b in (0, 2, 4, 6):
                            vector.wait_ge(sP.h, cP(j, "tr", 0, 4 * (b + 1) - 1))
                            if first and j >= 1:
                                vector.wait_ge(sSt.h, bS + 16 * j)
                                first = False
                            for kk in range(4):
                                k = 4 * b + kk
                                IN(vector.tensor_copy(
                                    nst[:, 128 * k:128 * (k + 1)],
                                    ps_tn[0][:, 128 * kk:128 * (kk + 1)]), sV)

                @blk.scalar
                def _(scalar):
                    for j in range(NCHE):
                        scalar.wait_ge(sV.h, cV(j, "red"))
                        scalar.activation(lnl2[:, :], ssq2[:, :], AF.Ln,
                                          scale=constv[:, 4:5],
                                          bias=constv[:, 0:1])
                        scalar.drain()
                        IN(scalar.activation(rstd2[:, :], lnl2[:, :], AF.Exp,
                                             scale=constv[:, 1:2]), sA)
                        if j >= 1:
                            scalar.wait_ge(sP.h, cP(j - 1, "tr", 0, 31))
                        scalar.drain()
                        IN(scalar.activation(nrm[:, :], g0[:, :], AF.Identity,
                                             scale=rstd2[:, :]), sA)
                        first = True
                        for b in (1, 3, 5, 7):
                            scalar.wait_ge(sP.h, cP(j, "tr", 0, 4 * (b + 1) - 1))
                            if first and j >= 1:
                                scalar.wait_ge(sSt.h, bS + 16 * j)
                                first = False
                            for kk in range(4):
                                k = 4 * b + kk
                                IN(scalar.copy(
                                    nst[:, 128 * k:128 * (k + 1)],
                                    ps_tn[1][:, 128 * kk:128 * (kk + 1)]), sA)
                        scalar.wait_ge(sV.h, cV(j, "ev", 0, 15))
                        scalar.drain()
                        IN(scalar.dma_start(
                            normed_sp[j, :, :, :].rearrange("fc p i -> p fc i"),
                            nst[:, :].rearrange("p (fc i) -> p fc i", fc=32)),
                            sSt, 16)
                        sA.v += 1
                        scalar.sem_inc(sA.h, 1)

                @blk.gpsimd
                def _(gpsimd):
                    for j in range(NCHE):
                        for g in range(NG):
                            gpsimd.wait_ge(sV.h, cV(j, "corrs", g))
                            gpsimd.wait_ge(sD[j % 2].h, corr_v[j])
                            if j >= 1:
                                gpsimd.wait_ge(sA.h, cA(j - 1, "normed"))
                            gpsimd.tensor_add(
                                g0[:, 512 * g:512 * (g + 1)],
                                corr[:, 512 * g:512 * (g + 1)],
                                ybuf[j % 2][:, 512 * g:512 * (g + 1)])
                            gpsimd.drain()
                            IN(gpsimd.tensor_mul(
                                g0[:, 512 * g:512 * (g + 1)],
                                g0[:, 512 * g:512 * (g + 1)],
                                sgbuf[j % 2][:, 512 * g:512 * (g + 1)]), sG)
            assert sP.v == bP + NPc * NCHE and sV.v == bV + NVc * NCHE
            assert sA.v == bA + NAc * NCHE and sG.v == bG + NGc * NCHE

        mid.close()

        with contextlib.ExitStack() as ph:
            L_ = ph.enter_context
            wsum_sb = L_(nc.sbuf_tensor("wsum_sb", [128, DC * D], BF16))
            biasb = L_(nc.sbuf_tensor("biasb", [128, D], F32))
            xnb = [L_(nc.sbuf_tensor(f"xnb{i}", [128, DC * 128], BF16))
                   for i in range(2)]
            mix = L_(nc.sbuf_tensor("mix", [128, 512], F32))
            gfst = [L_(nc.sbuf_tensor(f"gfst{i}", [128, D], BF16)) for i in range(2)]
            ps_f = [L_(nc.psum_tensor(f"ps_f{i}", [128, 512], F32)) for i in range(4)]
            bL, bV, bA, bP, bS = sL.v, sV.v, sA.v, sP.v, sSt.v
            st_tot = sSt.v
            xnb_v = []
            with nc.Block() as blk:
                @blk.sync
                def _(sync):
                    sync.wait_ge(sSt.h, st_tot)
                    for dc in range(DC):
                        IN(sync.dma_start(wsum_sb[:, dc * D:(dc + 1) * D],
                                          wsum_dram[dc, :, :]), sL, 16)
                    IN(sync.dma_start(biasb[:, :],
                                      fbias_d.broadcast_to([128, D])), sL, 16)
                    for j in range(NCHE):
                        if j >= 2:
                            sync.wait_ge(sP.h, bP + 4 * (j - 1))
                        xnb_v.append(IN(sync.dma_start(
                            xnb[j % 2][:, :].rearrange("p (dc i) -> p dc i", dc=DC),
                            xn_sp[:, :, PRE + 128 * j:PRE + 128 * (j + 1)].rearrange(
                                "dc p i -> p dc i")), sD[j % 2], 16))

                @blk.tensor
                def _(tensor):
                    tensor.wait_ge(sL.h, bL + 16 * 17)
                    for j in range(NCHE):
                        tensor.wait_ge(sD[j % 2].h, xnb_v[j])
                        for fq in range(4):
                            if j >= 1:
                                tensor.wait_ge(sV.h, bV + 4 * (j - 1) + fq + 1)
                            for dc in range(DC):
                                IN(tensor.matmul(
                                    ps_f[fq][:, :],
                                    xnb[j % 2][:, 128 * dc:128 * (dc + 1)],
                                    wsum_sb[:, dc * D + 512 * fq:
                                            dc * D + 512 * (fq + 1)],
                                    start=(dc == 0), stop=(dc == 15)),
                                    sP, 1 if dc == 15 else 0)

                @blk.vector
                def _(vector):
                    for j in range(NCHE):
                        for fq in range(4):
                            vector.wait_ge(sP.h, bP + 4 * j + fq + 1)
                            if fq >= 1:
                                vector.wait_ge(sA.h, bA + 5 * j + fq)
                            elif j >= 1:
                                vector.wait_ge(sA.h, bA + 5 * (j - 1) + 4)
                            IN(vector.scalar_tensor_tensor(
                                mix[:, :], ps_f[fq][:, :], RT8I,
                                biasb[:, 512 * fq:512 * (fq + 1)],
                                ALU.mult, ALU.add), sV)

                @blk.scalar
                def _(scalar):
                    for j in range(NCHE):
                        if j >= 2:
                            scalar.wait_ge(sSt.h, bS + 16 * j)
                        for fq in range(4):
                            scalar.wait_ge(sV.h, bV + 4 * j + fq + 1)
                            IN(scalar.activation(
                                gfst[j % 2][:, 512 * fq:512 * (fq + 1)],
                                mix[:, :], AF.Silu), sA)
                        scalar.drain()
                        IN(scalar.dma_start(gffn_sp[j, :, :], gfst[j % 2][:, :]),
                           sSt, 16)
                        sA.v += 1
                        scalar.sem_inc(sA.h, 1)
            assert sP.v == bP + 4 * NCHE and sV.v == bV + 4 * NCHE
            assert sA.v == bA + 5 * NCHE

        with contextlib.ExitStack() as ph:
            L_ = ph.enter_context
            wotmp = [L_(nc.sbuf_tensor(f"wotmp{i}", [128, 1024], F32))
                     for i in range(2)]
            wobf = L_(nc.sbuf_tensor("wobf", [128, 32 * 1024], BF16))
            nrmb = [L_(nc.sbuf_tensor(f"nrmb{i}", [128, INTER], BF16))
                    for i in range(2)]
            gfb = [L_(nc.sbuf_tensor(f"gfb{i}", [128, 1024], BF16)) for i in range(2)]
            xrb = [L_(nc.sbuf_tensor(f"xrb{i}", [128, 1024], F32)) for i in range(2)]
            tmo = L_(nc.sbuf_tensor("tmo", [128, 512], F32))
            outsb = [L_(nc.sbuf_tensor(f"outsb{i}", [128, 1024], F32))
                     for i in range(2)]
            ps_s = [L_(nc.psum_tensor(f"ps_s{i}", [128, 512], F32)) for i in range(4)]
            st_tot2 = sSt.v
            for dop in range(2):
                bV, bP, bG, bO = sV.v, sP.v, sG.v, sO.v
                wo_v = []
                ck_v = []
                with nc.Block() as blk:
                    @blk.sync
                    def _(sync):
                        if dop == 0:
                            sync.wait_ge(sSt.h, st_tot2)
                        for fc in range(32):
                            if fc >= 2:
                                sync.wait_ge(sG.h, bG + fc - 1)
                            elif dop == 1:
                                sync.wait_ge(sG.h, bG)
                            wo_v.append(IN(sync.dma_start(
                                wotmp[fc % 2][:, :],
                                wout_d[128 * fc:128 * (fc + 1),
                                       1024 * dop:1024 * (dop + 1)]),
                                sD[fc % 2], 16))
                        for j in range(NCHE):
                            if j >= 2:
                                sync.wait_ge(sP.h, bP + 2 * (j - 1))
                                sync.wait_ge(sV.h, bV + 2 * (j - 1))
                            IN(sync.dma_start(
                                nrmb[j % 2][:, :].rearrange(
                                    "p (fc i) -> p fc i", fc=32),
                                normed_sp[j, :, :, :].rearrange(
                                    "fc p i -> p fc i")), sD[2 + j % 2], 16)
                            IN(sync.dma_start(
                                gfb[j % 2][:, :],
                                gffn_sp[j, :, 1024 * dop:1024 * (dop + 1)]),
                               sD[2 + j % 2], 16)
                            ck_v.append(IN(sync.dma_start(
                                xrb[j % 2][:, :],
                                x_in[PRE + 128 * j:PRE + 128 * (j + 1),
                                     1024 * dop:1024 * (dop + 1)]),
                                sD[2 + j % 2], 16))

                    @blk.gpsimd
                    def _(gpsimd):
                        for fc in range(32):
                            gpsimd.wait_ge(sD[fc % 2].h, wo_v[fc])
                            if dop == 1 and fc == 0:
                                gpsimd.wait_ge(sP.h, bP)
                            IN(gpsimd.tensor_scalar_mul(
                                wobf[:, 1024 * fc:1024 * (fc + 1)],
                                wotmp[fc % 2][:, :], normw_s[:, fc:fc + 1]), sG)

                    @blk.tensor
                    def _(tensor):
                        tensor.wait_ge(sG.h, bG + 32)
                        for j in range(NCHE):
                            tensor.wait_ge(sD[2 + j % 2].h, ck_v[j])
                            for do2 in range(2):
                                pb = 2 * (j % 2) + do2
                                if j >= 2:
                                    tensor.wait_ge(sV.h, bV + 2 * (j - 2) + do2 + 1)
                                for fc in range(32):
                                    IN(tensor.matmul(
                                        ps_s[pb][:, :],
                                        nrmb[j % 2][:, 128 * fc:128 * (fc + 1)],
                                        wobf[:, 1024 * fc + 512 * do2:
                                             1024 * fc + 512 * (do2 + 1)],
                                        start=(fc == 0), stop=(fc == 31)),
                                        sP, 1 if fc == 31 else 0)

                    @blk.vector
                    def _(vector):
                        for j in range(NCHE):
                            vector.wait_ge(sP.h, bP + 2 * (j + 1))
                            if j >= 2:
                                vector.wait_ge(sO.h, bO + 16 * j)
                            elif dop == 1:
                                vector.wait_ge(sO.h, bO)
                            ob = outsb[j % 2]
                            for do2 in range(2):
                                pb = 2 * (j % 2) + do2
                                vector.tensor_mul(
                                    tmo[:, :], ps_s[pb][:, :],
                                    gfb[j % 2][:, 512 * do2:512 * (do2 + 1)])
                                vector.drain()
                                IN(vector.tensor_add(
                                    ob[:, 512 * do2:512 * (do2 + 1)], tmo[:, :],
                                    xrb[j % 2][:, 512 * do2:512 * (do2 + 1)]), sV)
                                vector.drain()

                    @blk.scalar
                    def _(scalar):
                        for j in range(NCHE):
                            scalar.wait_ge(sV.h, bV + 2 * (j + 1))
                            IN(scalar.dma_start(
                                out_d[128 * j:128 * (j + 1),
                                      1024 * dop:1024 * (dop + 1)],
                                outsb[j % 2][:, :]), sO, 16)
                assert sP.v == bP + 2 * NCHE and sV.v == bV + 2 * NCHE
                assert sO.v == bO + 16 * NCHE

        with nc.Block() as blk:
            @blk.sync
            def _(sync):
                sync.wait_ge(sO.h, sO.v)

    return nc


def kernel(**inputs):
    ins = {k: np.asarray(v) for k, v in inputs.items()}
    x = ins["L0_attn_hidden_states"]
    B = x.shape[0]

    nc = build_nc()

    conv_w = ins["L0_attn_conv_w"].reshape(48, 128, 4).transpose(1, 0, 2) \
        .reshape(128, 192)
    conv_b = ins["L0_attn_conv_b"].reshape(48, 128).T
    normw = ins["L0_attn_norm_w"].reshape(32, 128).T

    base = {
        "wz": np.ascontiguousarray(ins["L0_attn_Wz"], dtype=np.float32),
        "wx": np.ascontiguousarray(ins["L0_attn_Wx"], dtype=np.float32),
        "wb": np.ascontiguousarray(ins["L0_attn_WB"], dtype=np.float32),
        "wc": np.ascontiguousarray(ins["L0_attn_WC"], dtype=np.float32),
        "wdt": np.ascontiguousarray(ins["L0_attn_Wdt"], dtype=np.float32),
        "convw": np.ascontiguousarray(conv_w, dtype=np.float32),
        "convb": np.ascontiguousarray(conv_b, dtype=np.float32),
        "dtb": np.ascontiguousarray(
            ins["L0_attn_dt_bias"].reshape(64, 1), dtype=np.float32),
        "alog": np.ascontiguousarray(
            ins["L0_attn_A_log"].reshape(64, 1), dtype=np.float32),
        "dvec": np.ascontiguousarray(
            ins["L0_attn_D"].reshape(1, 64), dtype=np.float32),
        "normw": np.ascontiguousarray(normw, dtype=np.float32),
        "wout": np.ascontiguousarray(ins["L0_attn_Wout"], dtype=np.float32),
        "factors": np.ascontiguousarray(ins["L0_ffn_factors"], dtype=np.float32),
        "fbias": np.ascontiguousarray(
            ins["L0_ffn_bias"].reshape(1, 2048), dtype=np.float32),
    }

    in_maps = []
    for c in range(8):
        b, hh = c // 2, c % 2
        if hh == 0:
            pre = np.zeros((PRE, D), dtype=np.float32)
        else:
            pre = x[b, T - PRE:T, :]
        xcore = np.concatenate([pre, x[b, T * hh:T * (hh + 1), :]], axis=0)
        flags = np.zeros((128, 2), dtype=np.float32)
        flags[:, 0] = 1.0 if hh == 0 else 0.0
        flags[:, 1] = 0.0 if hh == 0 else 1.0
        m = dict(base)
        m["x"] = np.ascontiguousarray(xcore, dtype=np.float32)
        m["flags"] = flags
        in_maps.append(m)

    res = run_bass_kernel_spmd(nc, in_maps, core_ids=list(range(8)))

    out = np.empty((B, 8192, D), dtype=np.float32)
    for c in range(8):
        b, hh = c // 2, c % 2
        out[b, T * hh:T * (hh + 1), :] = res.results[c]["out"]
    return out


# version 06 (final)
# speedup vs baseline: 1.0171x; 1.0171x over previous
import math
import contextlib
import numpy as np
import ml_dtypes
import concourse.bass as bass
import concourse.mybir as mybir
from concourse.bass_utils import run_bass_kernel_spmd

F32 = mybir.dt.float32
BF16 = mybir.dt.bfloat16
AF = mybir.ActivationFunctionType
ALU = mybir.AluOpType
AX = mybir.AxisListType

D = 2048
DC = 16
INTER = 4096
NH = 64
HD = 64
NG = 8
NS = 128
GN = 1024
CONV = 6144
NCC = 48
T = 4096
PRE = 128
TTOT = T + PRE
Q = 128
NCH = 32
NCHE = 32
EPS = 1e-5
RT8I = 1.0 / math.sqrt(8.0)

HALVES = [(0, PRE + 2048, 17, True), (PRE + 2048, 2048, 16, False)]


class Ctr:
    __slots__ = ("h", "v")

    def __init__(self, h):
        self.h = h
        self.v = 0


def build_nc():
    nc = bass.Bass("TRN2", target_bir_lowering=False)

    x_in = nc.dram_tensor("x", [TTOT, D], F32, kind="ExternalInput").ap()
    wz_d = nc.dram_tensor("wz", [D, INTER], F32, kind="ExternalInput").ap()
    wx_d = nc.dram_tensor("wx", [D, INTER], F32, kind="ExternalInput").ap()
    wb_d = nc.dram_tensor("wb", [D, GN], F32, kind="ExternalInput").ap()
    wc_d = nc.dram_tensor("wc", [D, GN], F32, kind="ExternalInput").ap()
    wdt_d = nc.dram_tensor("wdt", [D, NH], F32, kind="ExternalInput").ap()
    convw_d = nc.dram_tensor("convw", [128, NCC * 4], F32, kind="ExternalInput").ap()
    convb_d = nc.dram_tensor("convb", [128, NCC], F32, kind="ExternalInput").ap()
    dtb_d = nc.dram_tensor("dtb", [NH, 1], F32, kind="ExternalInput").ap()
    alog_d = nc.dram_tensor("alog", [NH, 1], F32, kind="ExternalInput").ap()
    dvec_d = nc.dram_tensor("dvec", [1, NH], F32, kind="ExternalInput").ap()
    normw_d = nc.dram_tensor("normw", [128, 32], F32, kind="ExternalInput").ap()
    wout_d = nc.dram_tensor("wout", [INTER, D], F32, kind="ExternalInput").ap()
    fac_d = nc.dram_tensor("factors", [8, D, D], F32, kind="ExternalInput").ap()
    fbias_d = nc.dram_tensor("fbias", [1, D], F32, kind="ExternalInput").ap()
    flags_d = nc.dram_tensor("flags", [128, 2], F32, kind="ExternalInput").ap()
    out_d = nc.dram_tensor("out", [T, D], F32, kind="ExternalOutput").ap()

    wsum_dram = nc.dram_tensor("wsum_sc", [DC, 128, D], BF16).ap()
    xn_sp = nc.dram_tensor("xn_sp", [DC, 128, TTOT], BF16).ap()
    xconv_sp = nc.dram_tensor("xconv_sp", [NCH, NCC, 128, 128], BF16).ap()
    sgate_sp = nc.dram_tensor("sgate_sp", [NCH, 128, INTER], BF16).ap()
    y_sp = nc.dram_tensor("y_sp", [NCH, 128, INTER], BF16).ap()
    ecg_sp = nc.dram_tensor("ecg_sp", [NCH, 128, NH], F32).ap()
    gffn_sp = nc.dram_tensor("gffn_sp", [NCH, 128, D], BF16).ap()
    normed_sp = nc.dram_tensor("normed_sp", [NCH, 32, 128, 128], BF16).ap()
    cumflat_dr = nc.dram_tensor("cumflat", [1, NH * Q], F32).ap()
    cc_in = nc.dram_tensor("cc_in", [128, INTER], F32).ap()
    cc_out = nc.dram_tensor("cc_out", [128, INTER], F32).ap()

    id_np = np.eye(128, dtype=np.float32)
    ident_f = nc.inline_tensor(id_np, "ident_f").ap()
    ident_b = nc.inline_tensor(id_np.astype(ml_dtypes.bfloat16), "ident_b").ap()
    tri_np = np.triu(np.ones((128, 128), dtype=np.float32))
    tri_c = nc.inline_tensor(tri_np, "tri_c").ap()

    st = contextlib.ExitStack()
    with st:
        E = st.enter_context

        sL = Ctr(E(nc.semaphore("sL")))
        sFl = Ctr(E(nc.semaphore("sFl")))
        sSt = Ctr(E(nc.semaphore("sSt")))
        sYs = Ctr(E(nc.semaphore("sYs")))
        sEs = Ctr(E(nc.semaphore("sEs")))
        sO = Ctr(E(nc.semaphore("sO")))
        sP = Ctr(E(nc.semaphore("sP")))
        sV = Ctr(E(nc.semaphore("sV")))
        sA = Ctr(E(nc.semaphore("sA")))
        sG = Ctr(E(nc.semaphore("sG")))
        sC = Ctr(E(nc.semaphore("sC")))
        sD = [Ctr(E(nc.semaphore(f"sD{i}"))) for i in range(4)]
        sE = [Ctr(E(nc.semaphore(f"sE{i}"))) for i in range(2)]

        def IN(ins, c, k=1):
            c.v += k
            if k:
                ins.then_inc(c.h, k)
            return c.v

        def MARK(eng, c):
            eng.drain()
            c.v += 1
            eng.sem_inc(c.h, 1)
            return c.v

        identf = E(nc.sbuf_tensor("identf", [128, 128], F32))
        identb = E(nc.sbuf_tensor("identb", [128, 128], BF16))
        tri_s = E(nc.sbuf_tensor("tri_s", [128, 128], F32))
        convw_s = E(nc.sbuf_tensor("convw_s", [128, NCC * 4], F32))
        convb_s = E(nc.sbuf_tensor("convb_s", [128, NCC], F32))
        dtb_s = E(nc.sbuf_tensor("dtb_s", [NH, 1], F32))
        alog_s = E(nc.sbuf_tensor("alog_s", [NH, 1], F32))
        aneg_s = E(nc.sbuf_tensor("aneg_s", [NH, 1], F32))
        dbc_s = E(nc.sbuf_tensor("dbc_s", [128, NH], F32))
        flags_s = E(nc.sbuf_tensor("flags_s", [128, 2], F32))
        normw_s = E(nc.sbuf_tensor("normw_s", [128, 32], F32))
        constv = E(nc.sbuf_tensor("constv", [128, 6], F32))
        zero64 = E(nc.sbuf_tensor("zero64", [NH, Q], F32))
        mid = contextlib.ExitStack()
        M_ = mid.enter_context
        dt_s = M_(nc.sbuf_tensor("dt_s", [NH, T], F32))
        a_s = M_(nc.sbuf_tensor("a_s", [NH, T], F32))
        S_nd = M_(nc.sbuf_tensor("S_nd", [128, INTER], F32))
        S_bf = M_(nc.sbuf_tensor("S_bf", [128, INTER], BF16))
        dt_raw = M_(nc.sbuf_tensor("dt_raw", [NH, TTOT], F32))
        stash = M_(nc.sbuf_tensor("stash", [128, NCC * 6], F32))

        bA0 = sA.v
        with nc.Block() as blk:
            @blk.sync
            def _(sync):
                for dst, src in [
                    (identf[:, :], ident_f), (identb[:, :], ident_b),
                    (tri_s[:, :], tri_c), (convw_s[:, :], convw_d),
                    (convb_s[:, :], convb_d), (dtb_s[:, :], dtb_d),
                    (alog_s[:, :], alog_d), (flags_s[:, :], flags_d),
                    (normw_s[:, :], normw_d),
                    (dbc_s[:, :], dvec_d.broadcast_to([128, NH])),
                ]:
                    IN(sync.dma_start(dst, src), sL, 16)
                sync.wait_ge(sL.h, sL.v)

            initL = sL.v

            @blk.scalar
            def _(scalar):
                scalar.wait_ge(sL.h, initL)
                scalar.activation(alog_s[:, :], alog_s[:, :], AF.Exp)
                MARK(scalar, sA)

            @blk.vector
            def _(vector):
                vector.wait_ge(sL.h, initL)
                vector.memset(zero64[:, :], 0.0)
                vector.memset(constv[:, 0:1], EPS)
                vector.memset(constv[:, 1:2], -0.5)
                vector.memset(constv[:, 2:3], -1.0)
                vector.memset(constv[:, 3:4], 1.0)
                vector.memset(constv[:, 4:5], 1.0 / INTER)
                vector.memset(S_nd[:, :], 0.0)
                vector.memset(S_bf[:, :], 0.0)
                vector.wait_ge(sA.h, bA0 + 1)
                vector.tensor_scalar_mul(aneg_s[:, :], alog_s[:, :], -1.0)

        with contextlib.ExitStack() as ph:
            L_ = ph.enter_context
            fac_b = [L_(nc.sbuf_tensor(f"facb{i}", [128, D], F32)) for i in range(2)]
            facacc = L_(nc.sbuf_tensor("facacc", [128, D], F32))
            wsbf = L_(nc.sbuf_tensor("wsbf", [128, D], BF16))
            bV, bS = sV.v, sSt.v
            fac_v = []
            with nc.Block() as blk:
                @blk.sync
                def _(sync):
                    for dc in range(DC):
                        for r in range(8):
                            k = dc * 8 + r
                            if k >= 2:
                                dc2, r2 = divmod(k - 2, 8)
                                sync.wait_ge(sV.h, bV + dc2 * 9 + r2 + 1)
                            fac_v.append(IN(
                                sync.dma_start(fac_b[k % 2][:, :],
                                               fac_d[r, dc * 128:(dc + 1) * 128, :]),
                                sD[k % 2], 16))

                @blk.vector
                def _(vector):
                    for dc in range(DC):
                        if dc:
                            vector.wait_ge(sSt.h, bS + 16 * dc)
                        for r in range(8):
                            k = dc * 8 + r
                            vector.wait_ge(sD[k % 2].h, fac_v[k])
                            if r == 0:
                                IN(vector.tensor_copy(facacc[:, :], fac_b[k % 2][:, :]),
                                   sV)
                            else:
                                vector.drain()
                                IN(vector.tensor_add(facacc[:, :], facacc[:, :],
                                                     fac_b[k % 2][:, :]), sV)
                        vector.drain()
                        vector.tensor_copy(wsbf[:, :], facacc[:, :])
                        MARK(vector, sV)

                @blk.scalar
                def _(scalar):
                    for dc in range(DC):
                        scalar.wait_ge(sV.h, bV + 9 * dc + 9)
                        IN(scalar.dma_start(wsum_dram[dc, :, :], wsbf[:, :]), sSt, 16)
                    scalar.wait_ge(sSt.h, bS + 256)
            assert sV.v == bV + 144 and sSt.v == bS + 256

        tcnt = [0] * NCC

        for h, (C0, W, NCHK, HASPRE) in enumerate(HALVES):
            with contextlib.ExitStack() as ph:
                xn_fm = ph.enter_context(
                    nc.sbuf_tensor(f"xn_fm{h}", [128, DC * W], BF16))

                with contextlib.ExitStack() as lst:
                    L_ = lst.enter_context
                    xt = [L_(nc.sbuf_tensor(f"xt{h}_{i}", [128, D], F32)) for i in range(2)]
                    sq = L_(nc.sbuf_tensor(f"sq{h}", [128, D], F32))
                    xnt = L_(nc.sbuf_tensor(f"xnt{h}", [128, D], BF16))
                    s1 = L_(nc.sbuf_tensor(f"s1{h}", [128, 1], F32))
                    ssq = L_(nc.sbuf_tensor(f"ssq{h}", [128, 1], F32))
                    mean = L_(nc.sbuf_tensor(f"mean{h}", [128, 1], F32))
                    m2 = L_(nc.sbuf_tensor(f"m2{h}", [128, 1], F32))
                    var = L_(nc.sbuf_tensor(f"var{h}", [128, 1], F32))
                    lnl = L_(nc.sbuf_tensor(f"lnl{h}", [128, 1], F32))
                    rstd = L_(nc.sbuf_tensor(f"rstd{h}", [128, 1], F32))
                    mb = L_(nc.sbuf_tensor(f"mb{h}", [128, 1], F32))
                    ps_t = [L_(nc.psum_tensor(f"ps_t{h}_{i}", [128, 512], F32))
                            for i in range(2)]
                    bV, bA, bP = sV.v, sA.v, sP.v
                    NV, NA, NP = 10, 10, 16
                    xt_v = []
                    with nc.Block() as blk:
                        @blk.sync
                        def _(sync):
                            for i in range(NCHK):
                                if i >= 2:
                                    sync.wait_ge(sA.h, bA + NA * (i - 2) + 2)
                                xt_v.append(IN(sync.dma_start(
                                    xt[i % 2][:, :],
                                    x_in[C0 + i * 128: C0 + (i + 1) * 128, :]),
                                    sD[i % 2], 16))

                        @blk.tensor
                        def _(tensor):
                            for i in range(NCHK):
                                tensor.wait_ge(sA.h, bA + NA * i + 2)
                                for dc in range(DC):
                                    b = dc // 4
                                    if dc % 4 == 0:
                                        pb, pi = (b - 2, i) if b >= 2 else (b + 2, i - 1)
                                        if pi >= 0:
                                            lastev = 4 * (pb // 2) + 4
                                            if pb % 2 == 0:
                                                tensor.wait_ge(
                                                    sV.h, bV + NV * pi + 2 + lastev)
                                            else:
                                                tensor.wait_ge(
                                                    sA.h, bA + NA * pi + 2 + lastev)
                                    IN(tensor.matmul(
                                        ps_t[b % 2][:, 128 * (dc % 4):
                                                    128 * (dc % 4 + 1)],
                                        xnt[:, 128 * dc:128 * (dc + 1)],
                                        identb[:, :], start=True, stop=True), sP)

                        @blk.vector
                        def _(vector):
                            for i in range(NCHK):
                                vector.wait_ge(sD[i % 2].h, xt_v[i])
                                vector.reduce_sum(s1[:, :], xt[i % 2][:, :], axis=AX.X)
                                vector.scalar_tensor_tensor(
                                    sq[:, :], xt[i % 2][:, :], 1.0, xt[i % 2][:, :],
                                    ALU.bypass, ALU.mult, accum_out=ssq[:, :])
                                vector.drain()
                                vector.tensor_scalar_mul(mean[:, :], s1[:, :], 1.0 / D)
                                vector.drain()
                                vector.tensor_mul(m2[:, :], mean[:, :], mean[:, :])
                                vector.drain()
                                IN(vector.scalar_tensor_tensor(
                                    var[:, :], ssq[:, :], 1.0 / D, m2[:, :],
                                    ALU.mult, ALU.subtract), sV)
                                vector.wait_ge(sA.h, bA + NA * i + 1)
                                IN(vector.scalar_tensor_tensor(
                                    mb[:, :], mean[:, :], -1.0, rstd[:, :],
                                    ALU.mult, ALU.mult), sV)
                                for b in (0, 2):
                                    vector.wait_ge(sP.h,
                                                   bP + NP * i + 4 * (b + 1))
                                    for k in range(4):
                                        dc = 4 * b + k
                                        IN(vector.tensor_copy(
                                            xn_fm[:, dc * W + i * 128:
                                                  dc * W + (i + 1) * 128],
                                            ps_t[0][:, 128 * k:128 * (k + 1)]),
                                            sV)

                        @blk.scalar
                        def _(scalar):
                            for i in range(NCHK):
                                scalar.wait_ge(sV.h, bV + NV * i + 1)
                                scalar.activation(lnl[:, :], var[:, :], AF.Ln,
                                                  bias=constv[:, 0:1])
                                scalar.drain()
                                IN(scalar.activation(rstd[:, :], lnl[:, :], AF.Exp,
                                                     scale=constv[:, 1:2]), sA)
                                scalar.wait_ge(sV.h, bV + NV * i + 2)
                                scalar.drain()
                                IN(scalar.activation(xnt[:, :], xt[i % 2][:, :],
                                                     AF.Identity, bias=mb[:, :],
                                                     scale=rstd[:, :]), sA)
                                for b in (1, 3):
                                    scalar.wait_ge(sP.h,
                                                   bP + NP * i + 4 * (b + 1))
                                    for k in range(4):
                                        dc = 4 * b + k
                                        IN(scalar.copy(
                                            xn_fm[:, dc * W + i * 128:
                                                  dc * W + (i + 1) * 128],
                                            ps_t[1][:, 128 * k:128 * (k + 1)]),
                                            sA)
                    assert sV.v == bV + NV * NCHK and sA.v == bA + NA * NCHK
                    assert sP.v == bP + NP * NCHK

                if HASPRE:
                    tiles = [(0, 128)] + [(128 + 512 * k, 512) for k in range(4)]
                else:
                    tiles = [(512 * k, 512) for k in range(4)]
                NTI = len(tiles)
                with contextlib.ExitStack() as pst:
                    L_ = pst.enter_context
                    wtmp = [L_(nc.sbuf_tensor(f"wtmp{h}_{i}", [128, D], F32))
                            for i in range(2)]
                    wbf = [L_(nc.sbuf_tensor(f"wbf{h}_{i}", [128, D], BF16))
                           for i in range(2)]
                    acc_a = [L_(nc.sbuf_tensor(f"acc_a{h}_{i}", [128, 512], F32))
                             for i in range(2)]
                    acc_b = [L_(nc.sbuf_tensor(f"acc_b{h}_{i}", [128, 512], F32))
                             for i in range(2)]
                    xcbuf = [L_(nc.sbuf_tensor(f"xcbuf{h}_{i}", [128, 512], BF16))
                             for i in range(2)]
                    ps_x = [L_(nc.psum_tensor(f"ps_x{h}_{i}", [128, 512], F32))
                            for i in range(4)]
                    bV, bA, bP, bG, bS = sV.v, sA.v, sP.v, sG.v, sSt.v
                    wt_v = []
                    NXS = 16

                    def wsrc(cc):
                        if cc < 32:
                            return wx_d[:, 128 * cc:128 * (cc + 1)]
                        if cc < 40:
                            return wb_d[:, 128 * (cc - 32):128 * (cc - 31)]
                        return wc_d[:, 128 * (cc - 40):128 * (cc - 39)]

                    act_per_tile = [0 if (HASPRE and t == 0) else 2
                                    for t in range(NTI)]
                    actpre = [0]
                    for cc in range(NCC):
                        for t in range(NTI):
                            actpre.append(actpre[-1] + act_per_tile[t])

                    with nc.Block() as blk:
                        @blk.sync
                        def _(sync):
                            for dc in range(DC):
                                IN(sync.dma_start(
                                    xn_sp[dc, :, C0:C0 + W],
                                    xn_fm[:, dc * W:(dc + 1) * W]), sSt, 16)
                            for cc in range(NCC):
                                if cc >= 2:
                                    sync.wait_ge(sG.h, bG + cc - 1)
                                wt_v.append(IN(sync.dma_start(
                                    wtmp[cc % 2][:, :].rearrange(
                                        "p (dc f) -> p dc f", dc=DC),
                                    wsrc(cc).rearrange("(dc p) f -> p dc f", p=128)),
                                    sD[cc % 2], 16))

                        @blk.gpsimd
                        def _(gpsimd):
                            for cc in range(NCC):
                                gpsimd.wait_ge(sD[cc % 2].h, wt_v[cc])
                                if cc >= 2:
                                    gpsimd.wait_ge(sP.h, bP + (cc - 1) * NTI)
                                IN(gpsimd.tensor_copy(wbf[cc % 2][:, :],
                                                      wtmp[cc % 2][:, :]), sG)

                        @blk.tensor
                        def _(tensor):
                            for cc in range(NCC):
                                tensor.wait_ge(sG.h, bG + cc + 1)
                                for t, (tc0, tw) in enumerate(tiles):
                                    idx = cc * NTI + t
                                    if idx >= 4:
                                        tensor.wait_ge(sV.h, bV + idx - 3)
                                    pp = idx % 4
                                    for dc in range(DC):
                                        IN(tensor.matmul(
                                            ps_x[pp][:, 0:tw],
                                            wbf[cc % 2][:, 128 * dc:128 * (dc + 1)],
                                            xn_fm[:, dc * W + tc0:dc * W + tc0 + tw],
                                            start=(dc == 0), stop=(dc == 15)),
                                            sP, 1 if dc == 15 else 0)

                        @blk.scalar
                        def _(scalar):
                            xb = 0
                            for cc in range(NCC):
                                for t, (tc0, tw) in enumerate(tiles):
                                    idx = cc * NTI + t
                                    pp = idx % 4
                                    ap = idx % 2
                                    if HASPRE and t == 0:
                                        continue
                                    scalar.wait_ge(sP.h, bP + idx + 1)
                                    if idx >= 2:
                                        scalar.wait_ge(sV.h, bV + idx - 1)
                                    IN(scalar.activation(
                                        acc_a[ap][:, 0:tw], ps_x[pp][:, 0:tw],
                                        AF.Identity,
                                        scale=convw_s[:, cc * 4 + 3:cc * 4 + 4],
                                        bias=convb_s[:, cc:cc + 1]), sA)
                                    scalar.wait_ge(sV.h, bV + idx + 1)
                                    if xb >= 2:
                                        scalar.wait_ge(sE[xb % 2].h, sE[xb % 2].v)
                                    buf = xcbuf[xb % 2]
                                    scalar.activation(buf[:, 0:tw],
                                                      acc_b[ap][:, 0:tw], AF.Silu)
                                    scalar.drain()
                                    gcol = C0 + tc0
                                    ch0 = (gcol - PRE) // 128
                                    nch = tw // 128
                                    IN(scalar.dma_start(
                                        xconv_sp[ch0:ch0 + nch, cc, :, :]
                                        .rearrange("c p i -> p c i"),
                                        buf[:, 0:tw].rearrange(
                                            "p (c i) -> p c i", c=nch)),
                                        sE[xb % 2], 16)
                                    sA.v += 1
                                    scalar.sem_inc(sA.h, 1)
                                    xb += 1
                            scalar.wait_ge(sE[0].h, sE[0].v) if cc == NCC - 1 else None
                            scalar.wait_ge(sE[1].h, sE[1].v) if cc == NCC - 1 else None

                        @blk.vector
                        def _(vector):
                            for cc in range(NCC):
                                for t, (tc0, tw) in enumerate(tiles):
                                    idx = cc * NTI + t
                                    pp = idx % 4
                                    ap = idx % 2
                                    par = tcnt[cc] % 2
                                    npar = (tcnt[cc] + 1) % 2
                                    sb0 = cc * 6 + par * 3
                                    nb0 = cc * 6 + npar * 3
                                    if HASPRE and t == 0:
                                        vector.wait_ge(sP.h, bP + idx + 1)
                                        vector.tensor_copy(stash[:, nb0:nb0 + 3],
                                                           ps_x[pp][:, tw - 3:tw])
                                        MARK(vector, sV)
                                    else:
                                        vector.wait_ge(sA.h, bA + actpre[idx] + 1)
                                        srcdst = [(acc_a[ap], acc_b[ap]),
                                                  (acc_b[ap], acc_a[ap]),
                                                  (acc_a[ap], acc_b[ap])]
                                        for ki, (sft, k) in enumerate(
                                                [(1, 2), (2, 1), (3, 0)]):
                                            prev, cur = srcdst[ki]
                                            wcol = convw_s[:, cc * 4 + k:
                                                           cc * 4 + k + 1]
                                            vector.scalar_tensor_tensor(
                                                cur[:, sft:tw],
                                                ps_x[pp][:, 0:tw - sft], wcol,
                                                prev[:, sft:tw], ALU.mult, ALU.add)
                                            vector.scalar_tensor_tensor(
                                                cur[:, 0:sft],
                                                stash[:, sb0 + 3 - sft:sb0 + 3],
                                                wcol, prev[:, 0:sft],
                                                ALU.mult, ALU.add)
                                            vector.drain()
                                        vector.tensor_copy(stash[:, nb0:nb0 + 3],
                                                           ps_x[pp][:, tw - 3:tw])
                                        MARK(vector, sV)
                                    tcnt[cc] += 1
                    assert sP.v == bP + NCC * NTI and sV.v == bV + NCC * NTI
                    assert sG.v == bG + NCC

                with contextlib.ExitStack() as pst:
                    L_ = pst.enter_context
                    wztmp = [L_(nc.sbuf_tensor(f"wztmp{h}_{i}", [128, 2 * 1024], F32))
                             for i in range(2)]
                    wzbf = L_(nc.sbuf_tensor(f"wzbf{h}", [128, DC * 1024], BF16))
                    sgst = [L_(nc.sbuf_tensor(f"sgst{h}_{i}", [128, 1024], BF16))
                            for i in range(2)]
                    ps_g = [L_(nc.psum_tensor(f"ps_g{h}_{i}", [128, 512], F32))
                            for i in range(4)]
                    bA, bP, bG, bS = sA.v, sP.v, sG.v, sSt.v
                    NTC = 16
                    wz_v = []
                    with nc.Block() as blk:
                        @blk.sync
                        def _(sync):
                            for g2 in range(4):
                                for ld in range(8):
                                    k = g2 * 8 + ld
                                    if k >= 2:
                                        sync.wait_ge(sG.h, bG + k - 1)
                                    wz_v.append(IN(sync.dma_start(
                                        wztmp[k % 2][:, :].rearrange(
                                            "p (dc f) -> p dc f", dc=2),
                                        wz_d[256 * ld:256 * (ld + 1),
                                             1024 * g2:1024 * (g2 + 1)].rearrange(
                                            "(dc p) f -> p dc f", p=128)),
                                        sD[k % 2], 16))

                        @blk.gpsimd
                        def _(gpsimd):
                            for g2 in range(4):
                                for ld in range(8):
                                    k = g2 * 8 + ld
                                    gpsimd.wait_ge(sD[k % 2].h, wz_v[k])
                                    if g2 >= 1 and ld == 0:
                                        gpsimd.wait_ge(sP.h, bP + NTC * g2)
                                    IN(gpsimd.tensor_copy(
                                        wzbf[:, (2 * ld) * 1024:(2 * ld + 2) * 1024],
                                        wztmp[k % 2][:, :]), sG)

                        @blk.tensor
                        def _(tensor):
                            for g2 in range(4):
                                tensor.wait_ge(sG.h, bG + 8 * (g2 + 1))
                                for tc in range(NTC):
                                    it = g2 * NTC + tc
                                    if it >= 2:
                                        tensor.wait_ge(sA.h, bA + it - 1)
                                    lc0 = (PRE if HASPRE else 0) + 128 * tc
                                    pb = 2 * (tc % 2)
                                    for dc in range(DC):
                                        for fq in range(2):
                                            IN(tensor.matmul(
                                                ps_g[pb + fq][:, :],
                                                xn_fm[:, dc * W + lc0:
                                                      dc * W + lc0 + 128],
                                                wzbf[:, dc * 1024 + 512 * fq:
                                                     dc * 1024 + 512 * (fq + 1)],
                                                start=(dc == 0), stop=(dc == 15)),
                                                sP,
                                                1 if (dc == 15 and fq == 1) else 0)

                        @blk.scalar
                        def _(scalar):
                            for g2 in range(4):
                                for tc in range(NTC):
                                    it = g2 * NTC + tc
                                    scalar.wait_ge(sP.h, bP + it + 1)
                                    if it >= 2:
                                        scalar.wait_ge(sE[it % 2].h, sE[it % 2].v)
                                    pb = 2 * (tc % 2)
                                    sgb = sgst[it % 2]
                                    for fq in range(2):
                                        scalar.activation(
                                            sgb[:, 512 * fq:512 * (fq + 1)],
                                            ps_g[pb + fq][:, :], AF.Silu)
                                    scalar.drain()
                                    tcg = h * 16 + tc
                                    IN(scalar.dma_start(
                                        sgate_sp[tcg, :, 1024 * g2:1024 * (g2 + 1)],
                                        sgb[:, :]), sE[it % 2], 16)
                                    sA.v += 1
                                    scalar.sem_inc(sA.h, 1)
                            if g2 == 3:
                                scalar.wait_ge(sE[0].h, sE[0].v)
                                scalar.wait_ge(sE[1].h, sE[1].v)
                    assert sP.v == bP + 4 * NTC and sA.v == bA + 4 * NTC

                with contextlib.ExitStack() as pst:
                    L_ = pst.enter_context
                    wdtmp = L_(nc.sbuf_tensor(f"wdtmp{h}", [128, DC * NH], F32))
                    wdtbf = L_(nc.sbuf_tensor(f"wdtbf{h}", [128, DC * NH], BF16))
                    ps_dt = L_(nc.psum_tensor(f"ps_dt{h}", [NH, 512], F32))
                    bL, bV, bP, bG = sL.v, sV.v, sP.v, sG.v
                    with nc.Block() as blk:
                        @blk.sync
                        def _(sync):
                            IN(sync.dma_start(
                                wdtmp[:, :].rearrange("p (dc f) -> p dc f", dc=DC),
                                wdt_d.rearrange("(dc p) f -> p dc f", p=128)), sL, 16)
                            sync.wait_ge(sSt.h, sSt.v)

                        @blk.gpsimd
                        def _(gpsimd):
                            gpsimd.wait_ge(sL.h, bL + 16)
                            IN(gpsimd.tensor_copy(wdtbf[:, :], wdtmp[:, :]), sG)

                        @blk.tensor
                        def _(tensor):
                            tensor.wait_ge(sG.h, bG + 1)
                            for t, (tc0, tw) in enumerate(tiles):
                                if t >= 1:
                                    tensor.wait_ge(sV.h, bV + t)
                                for dc in range(DC):
                                    IN(tensor.matmul(
                                        ps_dt[:, 0:tw],
                                        wdtbf[:, NH * dc:NH * (dc + 1)],
                                        xn_fm[:, dc * W + tc0:dc * W + tc0 + tw],
                                        start=(dc == 0), stop=(dc == 15)),
                                        sP, 1 if dc == 15 else 0)

                        @blk.vector
                        def _(vector):
                            for t, (tc0, tw) in enumerate(tiles):
                                vector.wait_ge(sP.h, bP + t + 1)
                                IN(vector.tensor_copy(
                                    dt_raw[:, C0 + tc0:C0 + tc0 + tw],
                                    ps_dt[:, 0:tw]), sV)
                    assert sP.v == bP + NTI and sV.v == bV + NTI

        with contextlib.ExitStack() as ph:
            L_ = ph.enter_context
            zb = L_(nc.sbuf_tensor("zb", [NH, T], F32))
            ab = L_(nc.sbuf_tensor("ab", [NH, T], F32))
            eb = L_(nc.sbuf_tensor("eb", [NH, T], F32))
            mx = L_(nc.sbuf_tensor("mx", [NH, T], F32))
            bV, bA = sV.v, sA.v
            with nc.Block() as blk:
                @blk.vector
                def _(vector):
                    vector.tensor_scalar_add(zb[:, :], dt_raw[:, PRE:], dtb_s[:, :])
                    vector.drain()
                    IN(vector.tensor_scalar(mx[:, :], zb[:, :], 0.0, None, ALU.max),
                       sV)
                    vector.wait_ge(sA.h, bA + 1)
                    vector.tensor_add(dt_s[:, :], mx[:, :], eb[:, :])
                    vector.drain()
                    vector.tensor_scalar_mul(a_s[:, :], dt_s[:, :], aneg_s[:, :])

                @blk.scalar
                def _(scalar):
                    scalar.wait_ge(sV.h, bV + 1)
                    scalar.activation(ab[:, :], zb[:, :], AF.Abs)
                    scalar.drain()
                    scalar.activation(eb[:, :], ab[:, :], AF.Exp, scale=constv[0:64, 2:3])
                    scalar.drain()
                    scalar.activation(eb[:, :], eb[:, :], AF.Ln, bias=constv[0:64, 3:4])
                    MARK(scalar, sA)

        with contextlib.ExitStack() as ph:
            L_ = ph.enter_context
            xc = [L_(nc.sbuf_tensor(f"xc{i}", [128, NCC * 128], BF16))
                  for i in range(2)]
            X_tm = L_(nc.sbuf_tensor("X_tm", [128, INTER], BF16))
            Xt = L_(nc.sbuf_tensor("Xt", [128, INTER], BF16))
            Xtw = L_(nc.sbuf_tensor("Xtw", [128, INTER], BF16))
            btm_sb = L_(nc.sbuf_tensor("btm_sb", [128, 1024], BF16))
            bcast = L_(nc.sbuf_tensor("bcast", [128, NH * Q], F32))
            Ein = L_(nc.sbuf_tensor("Ein", [128, 8 * Q], F32))
            Ebf = L_(nc.sbuf_tensor("Ebf", [128, 8 * Q], BF16))
            Wp = L_(nc.sbuf_tensor("Wp", [128, 8 * Q], BF16))
            gttri = L_(nc.sbuf_tensor("gttri", [128, 2 * Q], BF16))
            cum_fm = L_(nc.sbuf_tensor("cum_fm", [NH, Q], F32))
            cum_tm = L_(nc.sbuf_tensor("cum_tm", [128, NH], F32))
            dtb_tm = L_(nc.sbuf_tensor("dtb_tm", [128, NH], BF16))
            wdecay = L_(nc.sbuf_tensor("wdecay", [NH, Q], F32))
            wdec_tm = L_(nc.sbuf_tensor("wdec_tm", [128, NH], BF16))
            ecum = L_(nc.sbuf_tensor("ecum", [128, NH], F32))
            lam_bc = L_(nc.sbuf_tensor("lam_bc", [128, NH], F32))
            lprod = [L_(nc.sbuf_tensor(f"lprod{i}", [128, NH], F32))
                     for i in range(2)]
            ecg_t = L_(nc.sbuf_tensor("ecg_t", [128, NH], F32))
            Y_sb = L_(nc.sbuf_tensor("Y_sb", [128, INTER], F32))
            y_bf = L_(nc.sbuf_tensor("y_bf", [128, INTER], BF16))
            t1g = L_(nc.sbuf_tensor("t1g", [128, 512], F32))
            tgP = L_(nc.sbuf_tensor("tgP", [128, 512], F32))
            ps_tf = L_(nc.psum_tensor("ps_tf", [128, 320], F32))
            ps_tx = [L_(nc.psum_tensor(f"ps_tx{i}", [128, 512], F32))
                     for i in range(2)]
            ps_gt = [L_(nc.psum_tensor(f"ps_gt{i}", [128, 128], F32))
                     for i in range(2)]
            ps_yi = L_(nc.psum_tensor("ps_yi", [128, 512], F32))
            ps_ya = L_(nc.psum_tensor("ps_ya", [128, 512], F32))
            ps_cb = L_(nc.psum_tensor("ps_cb", [128, 512], F32))

            bL, bV, bA, bP, bG = sL.v, sV.v, sA.v, sP.v, sG.v
            bFl, bYs, bEs = sFl.v, sYs.v, sEs.v
            stpre = sSt.v
            NPch, NVch, NAch, NGch = 75, 34, 51, 49

            def pP(j, name, g=0, k=0):
                basei = {"cumtr": 1, "dttr": 2, "wdtr": 3}
                if name in basei:
                    return bP + NPch * j + basei[name]
                off = {"gt": 1, "xtr": 2 + k, "btr": 6, "inter": 7, "intra": 8,
                       "ctrb": 9}
                return bP + NPch * j + 3 + 9 * g + off[name]

            def pV(j, name, g=0):
                if name == "cum":
                    return bV + NVch * j + 1
                off = {"gttri": 1, "mask": 2, "t1y": 3, "rec": 4}
                if name in off:
                    return bV + NVch * j + 1 + 4 * g + off[name]
                return bV + NVch * j + 34

            def pA(j, name, g=0, k=0):
                basei = {"wdecay": 1, "evacall": 2, "ecumlam": 3}
                if name in basei:
                    return bA + NAch * j + basei[name]
                if name == "xev":
                    return bA + NAch * j + 3 + 6 * g + k + 1
                return bA + NAch * j + 3 + 6 * g + 6

            def pG(j, name, g=0):
                if name in ("xs", "xws", "mask", "wp", "sl"):
                    off = {"xs": 1, "xws": 2, "mask": 3, "wp": 4, "sl": 5}
                    return bG + NGch * j + 5 * g + off[name]
                if name == "sbf":
                    return bG + NGch * j + 40 + g + 1
                if name == "dterm":
                    return bG + NGch * j + 49

            xcload_v = []
            repl_v = []

            with nc.Block() as blk:
                @blk.sync
                def _(sync):
                    sync.wait_ge(sSt.h, stpre)
                    xcload_v.append(IN(sync.dma_start(
                        xc[0][:, :].rearrange("p (cc i) -> p cc i", cc=NCC),
                        xconv_sp[0, :, :, :].rearrange("cc p i -> p cc i")),
                        sD[0], 16))
                    for j in range(NCHE):
                        if j + 1 < NCHE:
                            if j >= 1:
                                sync.wait_ge(sP.h, bP + NPch * j)
                            xcload_v.append(IN(sync.dma_start(
                                xc[(j + 1) % 2][:, :].rearrange(
                                    "p (cc i) -> p cc i", cc=NCC),
                                xconv_sp[j + 1, :, :, :].rearrange(
                                    "cc p i -> p cc i")), sD[(j + 1) % 2], 16))
                        sync.wait_ge(sV.h, pV(j, "cum"))
                        if j >= 1:
                            sync.wait_ge(sD[2].h, repl_v[j - 1])
                        IN(sync.dma_start(cumflat_dr, cum_fm[:, :]), sFl, 16)
                        sync.wait_ge(sFl.h, bFl + 16 * (j + 1))
                        if j >= 1:
                            sync.wait_ge(sV.h, pV(j - 1, "mask", 7))
                            sync.wait_ge(sG.h, pG(j - 1, "mask", 7))
                            sync.wait_ge(sA.h, pA(j - 1, "ecumlam"))
                        repl_v.append(IN(sync.dma_start(
                            bcast[:, :],
                            cumflat_dr.broadcast_to([128, NH * Q])), sD[2], 16))

                @blk.tensor
                def _(tensor):
                    for j in range(NCHE):
                        tensor.wait_ge(sV.h, pV(j, "cum"))
                        if j >= 1:
                            tensor.wait_ge(sA.h, pA(j - 1, "xev", 7, 4))
                        IN(tensor.matmul(ps_tf[:, 0:64], cum_fm[:, :],
                                           identf[0:64, 0:64],
                                           start=True, stop=True), sP)
                        IN(tensor.matmul(ps_tf[:, 64:128],
                                         dt_s[:, Q * j:Q * (j + 1)],
                                         identf[0:64, 0:64],
                                         start=True, stop=True), sP)
                        tensor.wait_ge(sA.h, pA(j, "wdecay"))
                        IN(tensor.matmul(ps_tf[:, 128:192], wdecay[:, :],
                                         identf[0:64, 0:64],
                                         start=True, stop=True), sP)
                        tensor.wait_ge(sD[j % 2].h, xcload_v[j])
                        for g in range(NG):
                            if j >= 1 or g >= 2:
                                jj, gg = (j, g - 2) if g >= 2 else (j - 1, g + 6)
                                tensor.wait_ge(sV.h, pV(jj, "gttri", gg))
                            IN(tensor.matmul(
                                ps_gt[g % 2][:, :],
                                xc[j % 2][:, (32 + g) * 128:(33 + g) * 128],
                                xc[j % 2][:, (40 + g) * 128:(41 + g) * 128],
                                start=True, stop=True), sP)
                            for k in range(4):
                                if k == 0:
                                    if g >= 2:
                                        tensor.wait_ge(sA.h,
                                                       pA(j, "xev", g - 2, 3))
                                    elif j >= 1:
                                        tensor.wait_ge(sA.h,
                                                       pA(j - 1, "xev", g + 6, 3))
                                src_cc = 4 * g + k
                                IN(tensor.matmul(
                                    ps_tx[g % 2][:, 128 * k:128 * (k + 1)],
                                    xc[j % 2][:, src_cc * 128:(src_cc + 1) * 128],
                                    identb[:, :], start=True, stop=True), sP)
                            if g >= 1:
                                tensor.wait_ge(sA.h, pA(j, "xev", g - 1, 4))
                            else:
                                tensor.wait_ge(sA.h, pA(j, "evacall"))
                                if j >= 1:
                                    tensor.wait_ge(sA.h, pA(j - 1, "xev", 7, 4))
                            IN(tensor.matmul(
                                ps_tf[:, 192:320],
                                xc[j % 2][:, (32 + g) * 128:(33 + g) * 128],
                                identb[:, :], start=True, stop=True), sP)
                            if j >= 1:
                                tensor.wait_ge(sG.h, pG(j - 1, "sbf", g))
                            if g >= 1:
                                tensor.wait_ge(sV.h, pV(j, "t1y", g - 1))
                            elif j >= 1:
                                tensor.wait_ge(sV.h, pV(j - 1, "t1y", 7))
                            IN(tensor.matmul(
                                ps_yi[:, :],
                                xc[j % 2][:, (40 + g) * 128:(41 + g) * 128],
                                S_bf[:, 512 * g:512 * (g + 1)],
                                start=True, stop=True), sP)
                            tensor.wait_ge(sG.h, pG(j, "wp", g))
                            for jj in range(8):
                                IN(tensor.matmul(
                                    ps_ya[:, 64 * jj:64 * (jj + 1)],
                                    Wp[:, 128 * jj:128 * (jj + 1)],
                                    Xt[:, 512 * g + 64 * jj:
                                       512 * g + 64 * (jj + 1)],
                                    start=True, stop=True),
                                    sP, 1 if jj == 7 else 0)
                            tensor.wait_ge(sG.h, pG(j, "xws", g))
                            tensor.wait_ge(sA.h, pA(j, "xev", g, 4))
                            if g >= 1:
                                tensor.wait_ge(sV.h, pV(j, "rec", g - 1))
                            elif j >= 1:
                                tensor.wait_ge(sV.h, pV(j - 1, "rec", 7))
                            for p2 in range(4):
                                IN(tensor.matmul(
                                    ps_cb[:, 128 * p2:128 * (p2 + 1)],
                                    btm_sb[:, 128 * g:128 * (g + 1)],
                                    Xtw[:, 512 * g + 128 * p2:
                                        512 * g + 128 * (p2 + 1)],
                                    start=True, stop=True),
                                    sP, 1 if p2 == 3 else 0)

                @blk.vector
                def _(vector):
                    vector.memset(lprod[0][:, :], 1.0)
                    for j in range(NCHE):
                        if j >= 1:
                            vector.wait_ge(sFl.h, bFl + 16 * j)
                            vector.wait_ge(sP.h, pP(j - 1, "wdtr"))
                        vector.tensor_tensor_scan(
                            cum_fm[:, :], a_s[:, Q * j:Q * (j + 1)], zero64[:, :],
                            0.0, ALU.add, ALU.add)
                        MARK(vector, sV)
                        for g in range(NG):
                            vector.wait_ge(sP.h, pP(j, "gt", g))
                            if g >= 2:
                                vector.wait_ge(sG.h, pG(j, "wp", g - 2))
                            elif j >= 1:
                                vector.wait_ge(sG.h, pG(j - 1, "wp", g + 6))
                            IN(vector.tensor_mul(
                                gttri[:, 128 * (g % 2):128 * (g % 2 + 1)],
                                ps_gt[g % 2][:, :],
                                tri_s[:, :]), sV)
                            vector.wait_ge(sD[2].h, repl_v[j])
                            vector.wait_ge(sA.h, pA(j, "evacall"))
                            if g >= 1:
                                vector.wait_ge(sA.h, pA(j, "E", g - 1))
                            elif j >= 1:
                                vector.wait_ge(sA.h, pA(j - 1, "E", 7))
                            for jj in range(4):
                                hh = 8 * g + jj
                                vector.tensor_scalar(
                                    Ein[:, 128 * jj:128 * (jj + 1)],
                                    bcast[:, 128 * hh:128 * (hh + 1)],
                                    cum_tm[:, hh:hh + 1], 0.0,
                                    ALU.subtract, ALU.min)
                            MARK(vector, sV)
                            vector.wait_ge(sP.h, pP(j, "intra", g))
                            vector.wait_ge(sA.h, pA(j, "ecumlam"))
                            if g == 0 and j >= 1:
                                vector.wait_ge(sG.h, pG(j - 1, "dterm"))
                            vector.tensor_mul(
                                t1g[:, :].rearrange("p (a b) -> p a b", a=8),
                                ps_yi[:, :].rearrange("p (a b) -> p a b", a=8),
                                ecum[:, 8 * g:8 * (g + 1)].unsqueeze(2)
                                .broadcast_to([128, 8, 64]))
                            vector.drain()
                            IN(vector.tensor_add(
                                Y_sb[:, 512 * g:512 * (g + 1)], t1g[:, :],
                                ps_ya[:, :]), sV)
                            vector.wait_ge(sP.h, pP(j, "ctrb", g))
                            vector.wait_ge(sG.h, pG(j, "sl", g))
                            IN(vector.tensor_add(
                                S_nd[:, 512 * g:512 * (g + 1)], tgP[:, :],
                                ps_cb[:, :]), sV)
                        vector.wait_ge(sA.h, pA(j, "ecumlam"))
                        if j >= 1:
                            vector.wait_ge(sEs.h, bEs + 16 * j)
                        vector.tensor_mul(ecg_t[:, :], ecum[:, :],
                                          lprod[j % 2][:, :])
                        vector.drain()
                        vector.tensor_mul(lprod[(j + 1) % 2][:, :],
                                          lprod[j % 2][:, :], lam_bc[:, :])
                        MARK(vector, sV)

                @blk.scalar
                def _(scalar):
                    for j in range(NCHE):
                        scalar.wait_ge(sV.h, pV(j, "cum"))
                        IN(scalar.activation(wdecay[:, :], cum_fm[:, :], AF.Exp,
                                             scale=constv[0:64, 2:3],
                                             bias=cum_fm[:, Q - 1:Q]), sA)
                        scalar.wait_ge(sP.h, pP(j, "wdtr"))
                        scalar.copy(cum_tm[:, :], ps_tf[:, 0:64])
                        scalar.copy(dtb_tm[:, :], ps_tf[:, 64:128])
                        scalar.copy(wdec_tm[:, :], ps_tf[:, 128:192])
                        MARK(scalar, sA)
                        scalar.wait_ge(sD[2].h, repl_v[j])
                        scalar.drain()
                        scalar.activation(ecum[:, :], cum_tm[:, :], AF.Exp)
                        scalar.activation(lam_bc[:, :], bcast[:, Q - 1::Q], AF.Exp)
                        MARK(scalar, sA)
                        if j >= 1:
                            scalar.wait_ge(sP.h, bP + NPch * j)
                            scalar.wait_ge(sG.h, pG(j - 1, "dterm"))
                        for g in range(NG):
                            scalar.wait_ge(sP.h, pP(j, "btr", g))
                            for k in range(5):
                                if k == 4:
                                    IN(scalar.copy(
                                        btm_sb[:, 128 * g:128 * (g + 1)],
                                        ps_tf[:, 192:320]), sA)
                                else:
                                    cc = 4 * g + k
                                    IN(scalar.copy(
                                        X_tm[:, 128 * cc:128 * (cc + 1)],
                                        ps_tx[g % 2][:, 128 * k:128 * (k + 1)]),
                                        sA)
                            scalar.wait_ge(sV.h, pV(j, "mask", g))
                            scalar.wait_ge(sG.h, pG(j, "mask", g))
                            if g >= 1:
                                scalar.wait_ge(sG.h, pG(j, "wp", g - 1))
                            elif j >= 1:
                                scalar.wait_ge(sG.h, pG(j - 1, "wp", 7))
                            IN(scalar.activation(Ebf[:, :], Ein[:, :], AF.Exp), sA)

                @blk.gpsimd
                def _(gpsimd):
                    for j in range(NCHE):
                        for g in range(NG):
                            gpsimd.wait_ge(sA.h, pA(j, "xev", g, 3))
                            gpsimd.wait_ge(sA.h, pA(j, "evacall"))
                            if g == 0 and j >= 1:
                                gpsimd.wait_ge(sP.h, bP + NPch * j)
                            IN(gpsimd.tensor_mul(
                                Xt[:, 512 * g:512 * (g + 1)].rearrange(
                                    "p (a b) -> p a b", a=8),
                                X_tm[:, 512 * g:512 * (g + 1)].rearrange(
                                    "p (a b) -> p a b", a=8),
                                dtb_tm[:, 8 * g:8 * (g + 1)].unsqueeze(2)
                                .broadcast_to([128, 8, 64])), sG)
                            gpsimd.wait_ge(sA.h, pA(j, "evacall"))
                            gpsimd.drain()
                            IN(gpsimd.tensor_mul(
                                Xtw[:, 512 * g:512 * (g + 1)].rearrange(
                                    "p (a b) -> p a b", a=8),
                                Xt[:, 512 * g:512 * (g + 1)].rearrange(
                                    "p (a b) -> p a b", a=8),
                                wdec_tm[:, 8 * g:8 * (g + 1)].unsqueeze(2)
                                .broadcast_to([128, 8, 64])), sG)
                            gpsimd.wait_ge(sD[2].h, repl_v[j])
                            if g >= 1:
                                gpsimd.wait_ge(sA.h, pA(j, "E", g - 1))
                            elif j >= 1:
                                gpsimd.wait_ge(sA.h, pA(j - 1, "E", 7))
                            for jj in range(4, 8):
                                hh = 8 * g + jj
                                gpsimd.tensor_scalar(
                                    Ein[:, 128 * jj:128 * (jj + 1)],
                                    bcast[:, 128 * hh:128 * (hh + 1)],
                                    cum_tm[:, hh:hh + 1], 0.0,
                                    ALU.subtract, ALU.min)
                            MARK(gpsimd, sG)
                            gpsimd.wait_ge(sA.h, pA(j, "E", g))
                            gpsimd.wait_ge(sV.h, pV(j, "gttri", g))
                            if g >= 1:
                                gpsimd.wait_ge(sP.h, pP(j, "intra", g - 1))
                            elif j >= 1:
                                gpsimd.wait_ge(sP.h, pP(j - 1, "intra", 7))
                            IN(gpsimd.tensor_mul(
                                Wp[:, :].rearrange("p (a b) -> p a b", a=8),
                                Ebf[:, :].rearrange("p (a b) -> p a b", a=8),
                                gttri[:, 128 * (g % 2):128 * (g % 2 + 1)]
                                .unsqueeze(1).broadcast_to([128, 8, 128])), sG)
                            gpsimd.wait_ge(sA.h, pA(j, "ecumlam"))
                            if g >= 1:
                                gpsimd.wait_ge(sV.h, pV(j, "rec", g - 1))
                            elif j >= 1:
                                gpsimd.wait_ge(sV.h, pV(j - 1, "rec", 7))
                            IN(gpsimd.tensor_mul(
                                tgP[:, :].rearrange("p (a b) -> p a b", a=8),
                                S_nd[:, 512 * g:512 * (g + 1)].rearrange(
                                    "p (a b) -> p a b", a=8),
                                lam_bc[:, 8 * g:8 * (g + 1)].unsqueeze(2)
                                .broadcast_to([128, 8, 64])), sG)
                        for g in range(NG):
                            gpsimd.wait_ge(sV.h, pV(j, "rec", g))
                            gpsimd.wait_ge(sP.h, pP(j, "inter", g))
                            IN(gpsimd.tensor_copy(
                                S_bf[:, 512 * g:512 * (g + 1)],
                                S_nd[:, 512 * g:512 * (g + 1)]), sG)
                        gpsimd.wait_ge(sV.h, pV(j, "ecg"))
                        IN(gpsimd.dma_start(ecg_sp[j, :, :], ecg_t[:, :]), sEs, 16)
                        gpsimd.wait_ge(sV.h, pV(j, "t1y", 7))
                        gpsimd.wait_ge(sP.h, bP + NPch * (j + 1))
                        if j >= 1:
                            gpsimd.wait_ge(sYs.h, bYs + 16 * j)
                        gpsimd.drain()
                        gpsimd.tensor_mul(
                            Xt[:, :].rearrange("p (a b) -> p a b", a=NH),
                            X_tm[:, :].rearrange("p (a b) -> p a b", a=NH),
                            dbc_s[:, :].unsqueeze(2).broadcast_to([128, NH, 64]))
                        gpsimd.drain()
                        gpsimd.tensor_add(y_bf[:, :], Y_sb[:, :], Xt[:, :])
                        gpsimd.drain()
                        IN(gpsimd.dma_start(y_sp[j, :, :], y_bf[:, :]), sYs, 16)
                        sG.v += 1
                        gpsimd.sem_inc(sG.h, 1)

            assert sP.v == bP + NPch * NCHE
            assert sV.v == bV + NVch * NCHE
            assert sA.v == bA + NAch * NCHE
            assert sG.v == bG + NGch * NCHE

            bV2, bSt2, bL2 = sV.v, sSt.v, sL.v
            with nc.Block() as blk:
                @blk.vector
                def _(vector):
                    vector.tensor_scalar_mul(Y_sb[:, :], S_nd[:, :], flags_s[:, 0:1])
                    MARK(vector, sV)
                    vector.wait_ge(sL.h, bL2 + 16)
                    vector.drain()
                    IN(vector.tensor_scalar_mul(S_nd[:, :], S_nd[:, :],
                                                flags_s[:, 1:2]), sV)

                @blk.gpsimd
                def _(gpsimd):
                    gpsimd.wait_ge(sSt.h, bSt2 + 16)
                    IN(gpsimd.collective_compute(
                        "AllReduce", ALU.add,
                        replica_groups=[[0, 1], [2, 3], [4, 5], [6, 7]],
                        ins=[cc_in], outs=[cc_out]), sC, 1)
                    gpsimd.wait_ge(sV.h, bV2 + 2)
                    IN(gpsimd.tensor_copy(S_bf[:, :], S_nd[:, :]), sG)

                @blk.sync
                def _(sync):
                    sync.wait_ge(sV.h, bV2 + 1)
                    IN(sync.dma_start(cc_in, Y_sb[:, :]), sSt, 16)
                    sync.wait_ge(sC.h, 1)
                    IN(sync.dma_start(S_nd[:, :], cc_out), sL, 16)

        with contextlib.ExitStack() as ph:
            L_ = ph.enter_context
            ybuf = [L_(nc.sbuf_tensor(f"ybuf{i}", [128, INTER], BF16))
                    for i in range(2)]
            sgbuf = [L_(nc.sbuf_tensor(f"sgbuf{i}", [128, INTER], BF16))
                     for i in range(2)]
            ecgb = [L_(nc.sbuf_tensor(f"ecgb{i}", [128, NH], F32)) for i in range(2)]
            cbuf = [L_(nc.sbuf_tensor(f"cbuf{i}", [128, NG * 128], BF16))
                    for i in range(2)]
            corr = L_(nc.sbuf_tensor("corr", [128, INTER], F32))
            g0 = L_(nc.sbuf_tensor("g0", [128, INTER], F32))
            scr5 = L_(nc.sbuf_tensor("scr5", [128, 512], F32))
            ssqc = L_(nc.sbuf_tensor("ssqc", [128, NG], F32))
            ssq2 = L_(nc.sbuf_tensor("ssq2", [128, 1], F32))
            lnl2 = L_(nc.sbuf_tensor("lnl2", [128, 1], F32))
            rstd2 = L_(nc.sbuf_tensor("rstd2", [128, 1], F32))
            nrm = L_(nc.sbuf_tensor("nrm", [128, INTER], BF16))
            nst = L_(nc.sbuf_tensor("nst", [128, INTER], BF16))
            ps_co = [L_(nc.psum_tensor(f"ps_co{i}", [128, 512], F32))
                     for i in range(2)]
            ps_tn = [L_(nc.psum_tensor(f"ps_tn{i}", [128, 512], F32))
                     for i in range(2)]
            bV, bA, bP, bG, bS = sV.v, sA.v, sP.v, sG.v, sSt.v
            ys_tot, es_tot, st_tot = sYs.v, sEs.v, sSt.v
            xe_tot = [sE[0].v, sE[1].v]
            corr_v = []
            NPc, NVc, NAc, NGc = 40, 33, 19, 8

            def cP(j, name, g=0, k=0):
                if name == "corr":
                    return bP + NPc * j + g + 1
                return bP + NPc * j + 8 + k + 1

            def cV(j, name, g=0, k=0):
                if name == "corrs":
                    return bV + NVc * j + g + 1
                if name == "sumsq":
                    return bV + NVc * j + 8 + g + 1
                if name == "red":
                    return bV + NVc * j + 17
                return bV + NVc * j + 17 + k + 1

            def cA(j, name, k=0):
                if name == "rstd":
                    return bA + NAc * j + 1
                if name == "normed":
                    return bA + NAc * j + 2
                if name == "ev":
                    return bA + NAc * j + 2 + k + 1
                return bA + NAc * j + NAc

            def cG(j, g):
                return bG + NGc * j + g + 1

            with nc.Block() as blk:
                @blk.sync
                def _(sync):
                    sync.wait_ge(sYs.h, ys_tot)
                    sync.wait_ge(sEs.h, es_tot)
                    sync.wait_ge(sSt.h, st_tot)
                    sync.wait_ge(sE[0].h, xe_tot[0])
                    sync.wait_ge(sE[1].h, xe_tot[1])
                    for j in range(NCHE):
                        if j >= 2:
                            sync.wait_ge(sG.h, cG(j - 2, 7))
                            sync.wait_ge(sV.h, cV(j - 2, "red"))
                            sync.wait_ge(sP.h, cP(j - 2, "corr", 7))
                        IN(sync.dma_start(ybuf[j % 2][:, :], y_sp[j, :, :]),
                           sD[j % 2], 16)
                        IN(sync.dma_start(sgbuf[j % 2][:, :], sgate_sp[j, :, :]),
                           sD[j % 2], 16)
                        IN(sync.dma_start(ecgb[j % 2][:, :], ecg_sp[j, :, :]),
                           sD[j % 2], 16)
                        corr_v.append(IN(sync.dma_start(
                            cbuf[j % 2][:, :].rearrange("p (cc i) -> p cc i", cc=NG),
                            xconv_sp[j, 40:48, :, :].rearrange("cc p i -> p cc i")),
                            sD[j % 2], 16))

                @blk.tensor
                def _(tensor):
                    for j in range(NCHE):
                        tensor.wait_ge(sD[j % 2].h, corr_v[j])
                        for g in range(NG):
                            if j >= 1 or g >= 2:
                                jj, gg = (j, g - 2) if g >= 2 else (j - 1, g + 6)
                                tensor.wait_ge(sV.h, cV(jj, "corrs", gg))
                            IN(tensor.matmul(
                                ps_co[g % 2][:, :],
                                cbuf[j % 2][:, 128 * g:128 * (g + 1)],
                                S_bf[:, 512 * g:512 * (g + 1)],
                                start=True, stop=True), sP)
                        tensor.wait_ge(sA.h, cA(j, "normed"))
                        for k in range(32):
                            b = k // 4
                            if k % 4 == 0:
                                pb, pj = (b - 2, j) if b >= 2 else (b + 6, j - 1)
                                if pj >= 0:
                                    lastev = 4 * (pb // 2) + 3
                                    if pb % 2 == 0:
                                        tensor.wait_ge(sV.h,
                                                       cV(pj, "ev", 0, lastev))
                                    else:
                                        tensor.wait_ge(sA.h, cA(pj, "ev", lastev))
                            IN(tensor.matmul(
                                ps_tn[b % 2][:, 128 * (k % 4):128 * (k % 4 + 1)],
                                nrm[:, 128 * k:128 * (k + 1)], identb[:, :],
                                start=True, stop=True), sP)

                @blk.vector
                def _(vector):
                    for j in range(NCHE):
                        for g in range(NG):
                            vector.wait_ge(sP.h, cP(j, "corr", g))
                            if j >= 1:
                                vector.wait_ge(sG.h, cG(j - 1, g))
                            IN(vector.tensor_mul(
                                corr[:, 512 * g:512 * (g + 1)].rearrange(
                                    "p (a b) -> p a b", a=8),
                                ps_co[g % 2][:, :].rearrange(
                                    "p (a b) -> p a b", a=8),
                                ecgb[j % 2][:, 8 * g:8 * (g + 1)].unsqueeze(2)
                                .broadcast_to([128, 8, 64])), sV)
                        for g in range(NG):
                            vector.wait_ge(sG.h, cG(j, g))
                            vector.drain()
                            IN(vector.scalar_tensor_tensor(
                                scr5[:, :], g0[:, 512 * g:512 * (g + 1)], 1.0,
                                g0[:, 512 * g:512 * (g + 1)],
                                ALU.bypass, ALU.mult,
                                accum_out=ssqc[:, g:g + 1]), sV)
                        vector.drain()
                        IN(vector.reduce_sum(ssq2[:, :], ssqc[:, :], axis=AX.X), sV)
                        first = True
                        for b in (0, 2, 4, 6):
                            vector.wait_ge(sP.h, cP(j, "tr", 0, 4 * (b + 1) - 1))
                            if first and j >= 1:
                                vector.wait_ge(sSt.h, bS + 16 * j)
                                first = False
                            for kk in range(4):
                                k = 4 * b + kk
                                IN(vector.tensor_copy(
                                    nst[:, 128 * k:128 * (k + 1)],
                                    ps_tn[0][:, 128 * kk:128 * (kk + 1)]), sV)

                @blk.scalar
                def _(scalar):
                    for j in range(NCHE):
                        scalar.wait_ge(sV.h, cV(j, "red"))
                        scalar.activation(lnl2[:, :], ssq2[:, :], AF.Ln,
                                          scale=constv[:, 4:5],
                                          bias=constv[:, 0:1])
                        scalar.drain()
                        IN(scalar.activation(rstd2[:, :], lnl2[:, :], AF.Exp,
                                             scale=constv[:, 1:2]), sA)
                        if j >= 1:
                            scalar.wait_ge(sP.h, cP(j - 1, "tr", 0, 31))
                        scalar.drain()
                        IN(scalar.activation(nrm[:, :], g0[:, :], AF.Identity,
                                             scale=rstd2[:, :]), sA)
                        first = True
                        for b in (1, 3, 5, 7):
                            scalar.wait_ge(sP.h, cP(j, "tr", 0, 4 * (b + 1) - 1))
                            if first and j >= 1:
                                scalar.wait_ge(sSt.h, bS + 16 * j)
                                first = False
                            for kk in range(4):
                                k = 4 * b + kk
                                IN(scalar.copy(
                                    nst[:, 128 * k:128 * (k + 1)],
                                    ps_tn[1][:, 128 * kk:128 * (kk + 1)]), sA)
                        scalar.wait_ge(sV.h, cV(j, "ev", 0, 15))
                        scalar.drain()
                        IN(scalar.dma_start(
                            normed_sp[j, :, :, :].rearrange("fc p i -> p fc i"),
                            nst[:, :].rearrange("p (fc i) -> p fc i", fc=32)),
                            sSt, 16)
                        sA.v += 1
                        scalar.sem_inc(sA.h, 1)

                @blk.gpsimd
                def _(gpsimd):
                    for j in range(NCHE):
                        for g in range(NG):
                            gpsimd.wait_ge(sV.h, cV(j, "corrs", g))
                            gpsimd.wait_ge(sD[j % 2].h, corr_v[j])
                            if j >= 1:
                                gpsimd.wait_ge(sA.h, cA(j - 1, "normed"))
                            gpsimd.tensor_add(
                                g0[:, 512 * g:512 * (g + 1)],
                                corr[:, 512 * g:512 * (g + 1)],
                                ybuf[j % 2][:, 512 * g:512 * (g + 1)])
                            gpsimd.drain()
                            IN(gpsimd.tensor_mul(
                                g0[:, 512 * g:512 * (g + 1)],
                                g0[:, 512 * g:512 * (g + 1)],
                                sgbuf[j % 2][:, 512 * g:512 * (g + 1)]), sG)
            assert sP.v == bP + NPc * NCHE and sV.v == bV + NVc * NCHE
            assert sA.v == bA + NAc * NCHE and sG.v == bG + NGc * NCHE

        mid.close()

        with contextlib.ExitStack() as ph:
            L_ = ph.enter_context
            wsum_sb = L_(nc.sbuf_tensor("wsum_sb", [128, DC * D], BF16))
            biasb = L_(nc.sbuf_tensor("biasb", [128, D], F32))
            xnb = [L_(nc.sbuf_tensor(f"xnb{i}", [128, DC * 128], BF16))
                   for i in range(2)]
            mix = L_(nc.sbuf_tensor("mix", [128, 512], F32))
            gfst = [L_(nc.sbuf_tensor(f"gfst{i}", [128, D], BF16)) for i in range(2)]
            ps_f = [L_(nc.psum_tensor(f"ps_f{i}", [128, 512], F32)) for i in range(4)]
            bL, bV, bA, bP, bS = sL.v, sV.v, sA.v, sP.v, sSt.v
            st_tot = sSt.v
            xnb_v = []
            with nc.Block() as blk:
                @blk.sync
                def _(sync):
                    sync.wait_ge(sSt.h, st_tot)
                    for dc in range(DC):
                        IN(sync.dma_start(wsum_sb[:, dc * D:(dc + 1) * D],
                                          wsum_dram[dc, :, :]), sL, 16)
                    IN(sync.dma_start(biasb[:, :],
                                      fbias_d.broadcast_to([128, D])), sL, 16)
                    for j in range(NCHE):
                        if j >= 2:
                            sync.wait_ge(sP.h, bP + 4 * (j - 1))
                        xnb_v.append(IN(sync.dma_start(
                            xnb[j % 2][:, :].rearrange("p (dc i) -> p dc i", dc=DC),
                            xn_sp[:, :, PRE + 128 * j:PRE + 128 * (j + 1)].rearrange(
                                "dc p i -> p dc i")), sD[j % 2], 16))

                @blk.tensor
                def _(tensor):
                    tensor.wait_ge(sL.h, bL + 16 * 17)
                    for j in range(NCHE):
                        tensor.wait_ge(sD[j % 2].h, xnb_v[j])
                        for fq in range(4):
                            if j >= 1:
                                tensor.wait_ge(sV.h, bV + 4 * (j - 1) + fq + 1)
                            for dc in range(DC):
                                IN(tensor.matmul(
                                    ps_f[fq][:, :],
                                    xnb[j % 2][:, 128 * dc:128 * (dc + 1)],
                                    wsum_sb[:, dc * D + 512 * fq:
                                            dc * D + 512 * (fq + 1)],
                                    start=(dc == 0), stop=(dc == 15)),
                                    sP, 1 if dc == 15 else 0)

                @blk.vector
                def _(vector):
                    for j in range(NCHE):
                        for fq in range(4):
                            vector.wait_ge(sP.h, bP + 4 * j + fq + 1)
                            if fq >= 1:
                                vector.wait_ge(sA.h, bA + 5 * j + fq)
                            elif j >= 1:
                                vector.wait_ge(sA.h, bA + 5 * (j - 1) + 4)
                            IN(vector.scalar_tensor_tensor(
                                mix[:, :], ps_f[fq][:, :], RT8I,
                                biasb[:, 512 * fq:512 * (fq + 1)],
                                ALU.mult, ALU.add), sV)

                @blk.scalar
                def _(scalar):
                    for j in range(NCHE):
                        if j >= 2:
                            scalar.wait_ge(sSt.h, bS + 16 * j)
                        for fq in range(4):
                            scalar.wait_ge(sV.h, bV + 4 * j + fq + 1)
                            IN(scalar.activation(
                                gfst[j % 2][:, 512 * fq:512 * (fq + 1)],
                                mix[:, :], AF.Silu), sA)
                        scalar.drain()
                        IN(scalar.dma_start(gffn_sp[j, :, :], gfst[j % 2][:, :]),
                           sSt, 16)
                        sA.v += 1
                        scalar.sem_inc(sA.h, 1)
            assert sP.v == bP + 4 * NCHE and sV.v == bV + 4 * NCHE
            assert sA.v == bA + 5 * NCHE

        with contextlib.ExitStack() as ph:
            L_ = ph.enter_context
            wotmp = [L_(nc.sbuf_tensor(f"wotmp{i}", [128, 1024], F32))
                     for i in range(2)]
            wobf = L_(nc.sbuf_tensor("wobf", [128, 32 * 1024], BF16))
            nrmb = [L_(nc.sbuf_tensor(f"nrmb{i}", [128, INTER], BF16))
                    for i in range(2)]
            gfb = [L_(nc.sbuf_tensor(f"gfb{i}", [128, 1024], BF16)) for i in range(2)]
            xrb = [L_(nc.sbuf_tensor(f"xrb{i}", [128, 1024], F32)) for i in range(2)]
            tmo = L_(nc.sbuf_tensor("tmo", [128, 512], F32))
            outsb = [L_(nc.sbuf_tensor(f"outsb{i}", [128, 1024], F32))
                     for i in range(2)]
            ps_s = [L_(nc.psum_tensor(f"ps_s{i}", [128, 512], F32)) for i in range(4)]
            st_tot2 = sSt.v
            for dop in range(2):
                bV, bP, bG, bO = sV.v, sP.v, sG.v, sO.v
                wo_v = []
                ck_v = []
                with nc.Block() as blk:
                    @blk.sync
                    def _(sync):
                        if dop == 0:
                            sync.wait_ge(sSt.h, st_tot2)
                        for fc in range(32):
                            if fc >= 2:
                                sync.wait_ge(sG.h, bG + fc - 1)
                            elif dop == 1:
                                sync.wait_ge(sG.h, bG)
                            wo_v.append(IN(sync.dma_start(
                                wotmp[fc % 2][:, :],
                                wout_d[128 * fc:128 * (fc + 1),
                                       1024 * dop:1024 * (dop + 1)]),
                                sD[fc % 2], 16))
                        for j in range(NCHE):
                            if j >= 2:
                                sync.wait_ge(sP.h, bP + 2 * (j - 1))
                                sync.wait_ge(sV.h, bV + 2 * (j - 1))
                            IN(sync.dma_start(
                                nrmb[j % 2][:, :].rearrange(
                                    "p (fc i) -> p fc i", fc=32),
                                normed_sp[j, :, :, :].rearrange(
                                    "fc p i -> p fc i")), sD[2 + j % 2], 16)
                            IN(sync.dma_start(
                                gfb[j % 2][:, :],
                                gffn_sp[j, :, 1024 * dop:1024 * (dop + 1)]),
                               sD[2 + j % 2], 16)
                            ck_v.append(IN(sync.dma_start(
                                xrb[j % 2][:, :],
                                x_in[PRE + 128 * j:PRE + 128 * (j + 1),
                                     1024 * dop:1024 * (dop + 1)]),
                                sD[2 + j % 2], 16))

                    @blk.gpsimd
                    def _(gpsimd):
                        for fc in range(32):
                            gpsimd.wait_ge(sD[fc % 2].h, wo_v[fc])
                            if dop == 1 and fc == 0:
                                gpsimd.wait_ge(sP.h, bP)
                            IN(gpsimd.tensor_scalar_mul(
                                wobf[:, 1024 * fc:1024 * (fc + 1)],
                                wotmp[fc % 2][:, :], normw_s[:, fc:fc + 1]), sG)

                    @blk.tensor
                    def _(tensor):
                        tensor.wait_ge(sG.h, bG + 32)
                        for j in range(NCHE):
                            tensor.wait_ge(sD[2 + j % 2].h, ck_v[j])
                            for do2 in range(2):
                                pb = 2 * (j % 2) + do2
                                if j >= 2:
                                    tensor.wait_ge(sV.h, bV + 2 * (j - 2) + do2 + 1)
                                for fc in range(32):
                                    IN(tensor.matmul(
                                        ps_s[pb][:, :],
                                        nrmb[j % 2][:, 128 * fc:128 * (fc + 1)],
                                        wobf[:, 1024 * fc + 512 * do2:
                                             1024 * fc + 512 * (do2 + 1)],
                                        start=(fc == 0), stop=(fc == 31)),
                                        sP, 1 if fc == 31 else 0)

                    @blk.vector
                    def _(vector):
                        for j in range(NCHE):
                            vector.wait_ge(sP.h, bP + 2 * (j + 1))
                            if j >= 2:
                                vector.wait_ge(sO.h, bO + 16 * j)
                            elif dop == 1:
                                vector.wait_ge(sO.h, bO)
                            ob = outsb[j % 2]
                            for do2 in range(2):
                                pb = 2 * (j % 2) + do2
                                vector.tensor_mul(
                                    tmo[:, :], ps_s[pb][:, :],
                                    gfb[j % 2][:, 512 * do2:512 * (do2 + 1)])
                                vector.drain()
                                IN(vector.tensor_add(
                                    ob[:, 512 * do2:512 * (do2 + 1)], tmo[:, :],
                                    xrb[j % 2][:, 512 * do2:512 * (do2 + 1)]), sV)
                                vector.drain()

                    @blk.scalar
                    def _(scalar):
                        for j in range(NCHE):
                            scalar.wait_ge(sV.h, bV + 2 * (j + 1))
                            IN(scalar.dma_start(
                                out_d[128 * j:128 * (j + 1),
                                      1024 * dop:1024 * (dop + 1)],
                                outsb[j % 2][:, :]), sO, 16)
                assert sP.v == bP + 2 * NCHE and sV.v == bV + 2 * NCHE
                assert sO.v == bO + 16 * NCHE

        with nc.Block() as blk:
            @blk.sync
            def _(sync):
                sync.wait_ge(sO.h, sO.v)

    return nc


def kernel(**inputs):
    ins = {k: np.asarray(v) for k, v in inputs.items()}
    x = ins["L0_attn_hidden_states"]
    B = x.shape[0]

    nc = build_nc()

    conv_w = ins["L0_attn_conv_w"].reshape(48, 128, 4).transpose(1, 0, 2) \
        .reshape(128, 192)
    conv_b = ins["L0_attn_conv_b"].reshape(48, 128).T
    normw = ins["L0_attn_norm_w"].reshape(32, 128).T

    base = {
        "wz": np.ascontiguousarray(ins["L0_attn_Wz"], dtype=np.float32),
        "wx": np.ascontiguousarray(ins["L0_attn_Wx"], dtype=np.float32),
        "wb": np.ascontiguousarray(ins["L0_attn_WB"], dtype=np.float32),
        "wc": np.ascontiguousarray(ins["L0_attn_WC"], dtype=np.float32),
        "wdt": np.ascontiguousarray(ins["L0_attn_Wdt"], dtype=np.float32),
        "convw": np.ascontiguousarray(conv_w, dtype=np.float32),
        "convb": np.ascontiguousarray(conv_b, dtype=np.float32),
        "dtb": np.ascontiguousarray(
            ins["L0_attn_dt_bias"].reshape(64, 1), dtype=np.float32),
        "alog": np.ascontiguousarray(
            ins["L0_attn_A_log"].reshape(64, 1), dtype=np.float32),
        "dvec": np.ascontiguousarray(
            ins["L0_attn_D"].reshape(1, 64), dtype=np.float32),
        "normw": np.ascontiguousarray(normw, dtype=np.float32),
        "wout": np.ascontiguousarray(ins["L0_attn_Wout"], dtype=np.float32),
        "factors": np.ascontiguousarray(ins["L0_ffn_factors"], dtype=np.float32),
        "fbias": np.ascontiguousarray(
            ins["L0_ffn_bias"].reshape(1, 2048), dtype=np.float32),
    }

    in_maps = []
    for c in range(8):
        b, hh = c // 2, c % 2
        if hh == 0:
            pre = np.zeros((PRE, D), dtype=np.float32)
        else:
            pre = x[b, T - PRE:T, :]
        xcore = np.concatenate([pre, x[b, T * hh:T * (hh + 1), :]], axis=0)
        flags = np.zeros((128, 2), dtype=np.float32)
        flags[:, 0] = 1.0 if hh == 0 else 0.0
        flags[:, 1] = 0.0 if hh == 0 else 1.0
        m = dict(base)
        m["x"] = np.ascontiguousarray(xcore, dtype=np.float32)
        m["flags"] = flags
        in_maps.append(m)

    res = run_bass_kernel_spmd(nc, in_maps, core_ids=list(range(8)))

    out = np.empty((B, 8192, D), dtype=np.float32)
    for c in range(8):
        b, hh = c // 2, c % 2
        out[b, T * hh:T * (hh + 1), :] = res.results[c]["out"]
    return out
